# version 18
# baseline (speedup 1.0000x reference)
import numpy as np
import ml_dtypes
from contextlib import ExitStack

import concourse.bass as bass
import concourse.mybir as mybir
from concourse.bass_utils import run_bass_kernel_spmd

F32 = mybir.dt.float32
BF16 = mybir.dt.bfloat16
AF = mybir.ActivationFunctionType
ALU = mybir.AluOpType

P = 128
D = 2048
I = 8192
T = 1024
NT = D // P
NI = I // P
CH = 512
NCH = T // CH
EPS = 1e-5

SV_B_IN, SV_B_DEC, SV_DBIAS, SV_B_GAT, SV_B_OUT = 0, 16, 32, 48, 64
SV_SU_UG, SV_WS_DP, SV_ME, SV_MO = 80, 96, 112, 128
SV_WS_UP, SV_WS_GT, SV_SU_DP = 144, 208, 272
SV_BIAS2, SV_SUP, SV_WSU2, SV_SFC, SV_S0 = 336, 352, 368, 432, 448
SV_CONST = 464
SV_W = 468

R0, R1, R2 = 0, 8192, 16384
ARF_W = 17920
XB_OFF, Y0_OFF, AG_OFF = 0, 16384, 32768
U2_OFF = 16384
SPARE_OFF = 49152
ARB_W = 50176

ENGS = ("sync", "pool", "pe", "act", "dve")


class Sched:
    def __init__(self):
        self.items = {e: [] for e in ENGS}
        self.cnt = {}

    def n(self, sem):
        return self.cnt.get(sem, 0)

    def add(self, eng, fn, waits=(), inc=None, inc_by=1):
        w = {}
        for s, v in waits:
            if v > 0:
                w[s] = max(w.get(s, 0), v)
        after = None
        if inc is not None:
            self.cnt[inc] = self.cnt.get(inc, 0) + inc_by
            after = self.cnt[inc]
        self.items[eng].append((fn, sorted(w.items()), inc, inc_by))
        return after

    def replay(self, eng_name, eng_obj, sems):
        for fn, waits, inc, inc_by in self.items[eng_name]:
            for s, v in waits:
                eng_obj.wait_ge(sems[s], v)
            r = fn(eng_obj)
            if inc is not None:
                if r is None:
                    r = eng_obj.nop()
                r.then_inc(sems[inc], inc_by)


def f_tt(o, a, b, op):
    return lambda e, o=o, a=a, b=b, op=op: e.tensor_tensor(out=o, in0=a, in1=b, op=op)


def f_ts(o, a, s1, s2, op0, op1=None):
    if op1 is None:
        return lambda e, o=o, a=a, s1=s1: e.tensor_scalar(
            out=o, in0=a, scalar1=s1, scalar2=None, op0=ALU.mult)
    return lambda e, o=o, a=a, s1=s1, s2=s2, op0=op0, op1=op1: e.tensor_scalar(
        out=o, in0=a, scalar1=s1, scalar2=s2, op0=op0, op1=op1)


def f_stt(o, a, sc, b, op0, op1):
    return lambda e, o=o, a=a, sc=sc, b=b, op0=op0, op1=op1: e.scalar_tensor_tensor(
        out=o, in0=a, scalar=sc, in1=b, op0=op0, op1=op1)


def f_act(o, a, func, bias=None, scale=None):
    def run(e, o=o, a=a, func=func, bias=bias, scale=scale):
        kw = {}
        if bias is not None:
            kw["bias"] = bias
        if scale is not None:
            kw["scale"] = scale
        return e.activation(out=o, in_=a, func=func, **kw)
    return run


def f_scan(o, d0, d1, init, op0, op1):
    return lambda e, o=o, d0=d0, d1=d1, init=init, op0=op0, op1=op1: \
        e.tensor_tensor_scan(out=o, data0=d0, data1=d1, initial=init, op0=op0, op1=op1)


def f_copy(o, a):
    return lambda e, o=o, a=a: e.tensor_copy(o, a)


def f_mmgroup(out_ap, pairs):
    def run(e, out_ap=out_ap, pairs=pairs):
        last = len(pairs) - 1
        r = None
        for j, (l, rh) in enumerate(pairs):
            r = e.matmul(out_ap, l, rh, start=(j == 0), stop=(j == last))
        return r
    return run


def f_mm1(out_ap, l, rh, start=True, stop=True):
    return lambda e, o=out_ap, l=l, rh=rh, st=start, sp=stop: e.matmul(
        o, l, rh, start=st, stop=sp)


def f_memset(ap, val):
    return lambda e, ap=ap, val=val: e.memset(ap, val)


def hadamard(n):
    H = np.array([[1.0]], dtype=np.float64)
    while H.shape[0] < n:
        H = np.block([[H, H], [H, -H]])
    return H


class Bfly:

    def __init__(self, ntiles, locs, spare, pair_cb, final_cb, absorb_h1=True):
        self.n = ntiles
        self.loc = dict(locs)
        self.spare = spare
        self.pair_cb = pair_cb
        self.final_cb = final_cb
        self.absorb_h1 = absorb_h1
        self.done = set()
        self.arrived = set()
        self.final_tiles = set()
        self.fired = set()

    def _dep_ok(self, j, h):
        if 2 * h >= self.n:
            return j in self.arrived and (j + h) in self.arrived
        d0 = (j & ~(2 * h), 2 * h)
        d1 = ((j + h) & ~(2 * h), 2 * h)
        return d0 in self.done and d1 in self.done

    def stale_slot(self):
        lv = self.n.bit_length() - (2 if self.absorb_h1 else 1)
        assert len(self.done) == (self.n // 2) * lv
        return self.spare

    def tile_arrived(self, t):
        self.arrived.add(t)
        hmin = 2 if self.absorb_h1 else 1
        prog = True
        while prog:
            prog = False
            h = self.n // 2
            while h >= hmin:
                for j in range(self.n):
                    if (j & h) == 0 and (j, h) not in self.done and self._dep_ok(j, h):
                        la, lb = self.loc[j], self.loc[j + h]
                        self.pair_cb(la, lb, self.spare)
                        self.loc[j + h] = self.spare
                        self.spare = lb
                        self.done.add((j, h))
                        if h == hmin:
                            if self.absorb_h1:
                                self.final_tiles.add(j)
                                self.final_tiles.add(j + 2)
                                for p in (j & ~1, (j + 2) & ~1):
                                    if (p in self.final_tiles
                                            and p + 1 in self.final_tiles
                                            and p not in self.fired):
                                        self.fired.add(p)
                                        self.final_cb(p, self.loc[p],
                                                      self.loc[p + 1])
                            else:
                                self.final_cb(j, self.loc[j], None)
                                self.final_cb(j + h, self.loc[j + h], None)
                        prog = True
                h //= 2


def build_program():
    nc = bass.Bass("TRN2", target_bir_lowering=False, debug=False)

    xT_d = nc.dram_tensor("xT", [D, T], F32, kind="ExternalInput")
    w_ind_d = nc.dram_tensor("w_ind", [8, D, 512], F32, kind="ExternalInput")
    w_gat_d = nc.dram_tensor("w_gat", [4, D, 512], F32, kind="ExternalInput")
    w_out_d = nc.dram_tensor("w_out", [4, D, 512], F32, kind="ExternalInput")
    w_ug_d = nc.dram_tensor("w_ug", [32, D, 512], F32, kind="ExternalInput")
    w_dn_d = nc.dram_tensor("w_dn", [16, I, 128], F32, kind="ExternalInput")
    sv_d = nc.dram_tensor("sv", [P, SV_W], F32, kind="ExternalInput")
    h128_d = nc.dram_tensor("h128", [P, P], BF16, kind="ExternalInput")
    h128n_d = nc.dram_tensor("h128n", [P, P], BF16, kind="ExternalInput")
    out_d = nc.dram_tensor("out", [D, T], F32, kind="ExternalOutput")
    x1h_d = nc.dram_tensor("x1h", [D, CH], F32)
    u2h_d = nc.dram_tensor("u2h", [NI, P, CH], BF16)
    wc_ug_d = nc.dram_tensor("wc_ug", [32, P, 8192], BF16)
    wc_dn_d = nc.dram_tensor("wc_dn", [16, P, 8192], BF16)
    ccin_d = nc.dram_tensor("ccin", [D], F32)
    ccout_d = nc.dram_tensor("ccout", [D], F32)

    S = Sched()

    with ExitStack() as ctx:
        sbuf = lambda n, sh, dt: ctx.enter_context(nc.sbuf_tensor(n, sh, dt))
        psumt = lambda n, sh: ctx.enter_context(nc.psum_tensor(n, sh, F32))
        mksem = lambda n: ctx.enter_context(nc.semaphore(n))

        ARB = sbuf("ARB", [P, ARB_W], BF16)
        ARF = sbuf("ARF", [P, ARF_W], F32)
        WR = sbuf("WR", [P, 16384], BF16)
        SV = sbuf("SV", [P, SV_W], F32)
        H128 = sbuf("H128", [P, P], BF16)
        H128N = sbuf("H128N", [P, P], BF16)
        ONESB = sbuf("ONESB", [P, 1], BF16)
        ONESF = sbuf("ONESF", [P, 1], F32)
        ONES1 = sbuf("ONES1", [1, P], F32)
        STATR = sbuf("STATR", [1, 1024], F32)

        pb = [psumt(f"pb{i}", [P, 512]) for i in range(4)]
        pR = psumt("pR", [P, 512])
        pM = psumt("pM", [P, 512])
        pS0 = psumt("pS0", [1, 512])
        pS1 = psumt("pS1", [1, 512])

        sems = {n: mksem(n) for n in
                ("sSP", "sXB", "sW0", "sW1", "sW2", "sW3", "sXR0", "sXR1",
                 "sX1", "sX1S", "sCI", "sS0", "sST0", "sST1",
                 "sU0", "sU1", "sCC", "sPE", "sACT", "sDVE")}
        block = ctx.enter_context(nc.Block())

        arb = lambda off, w: ARB[:, off:off + w]
        arf = lambda off, w: ARF[:, off:off + w]
        XBs = lambda k, a, w: arb(XB_OFF + k * T + a, w)
        HBs = lambda k, c: arb(XB_OFF + k * T + c * CH, CH)
        Y0s = lambda k: arb(Y0_OFF + k * T, T)
        Y0c = lambda k, c: arb(Y0_OFF + k * T + c * CH, CH)
        AGs = lambda k: arb(AG_OFF + k * T, T)
        U2s = lambda i: arb(U2_OFF + i * CH, CH)
        STB = lambda m: arb(AG_OFF + m * CH, CH)
        STBq = lambda m: arb(AG_OFF + 8192 + m * CH, CH)
        svc = lambda off, g: SV[:, off + g:off + g + 1]

        def dma(eng, out_ap, in_ap, sem, waits=()):
            fn = lambda e, o=out_ap, i=in_ap: e.dma_start(out=o, in_=i)
            return S.add(eng, fn, waits=waits, inc=sem, inc_by=16)

        dma("sync", SV[:, :], sv_d[:, :], "sSP")
        dma("sync", H128[:, :], h128_d[:, :], "sSP")
        dma("sync", H128N[:, :], h128n_d[:, :], "sSP")
        sv_ld = ("sSP", 48)

        S.add("dve", f_memset(ONESB[:, :], 1.0), inc="sDVE")
        S.add("dve", f_memset(ONESF[:, :], 1.0), inc="sDVE")
        S.add("dve", f_memset(ONES1[:, :], 1.0), inc="sDVE")
        ones_done = S.n("sDVE")

        for half in range(2):
            dma("pool",
                ARB[:, XB_OFF + half * 8 * T: XB_OFF + (half + 1) * 8 * T]
                .rearrange("p (k t) -> p k t", k=8),
                xT_d[half * 8 * P:(half + 1) * 8 * P, :]
                .rearrange("(k p) t -> p k t", p=P), "sXB")
        xb_ld = ("sXB", 32)

        S.add("dve", f_tt(SV[:, SV_BIAS2:SV_BIAS2 + 16], SV[:, SV_B_DEC:SV_B_DEC + 16],
                          SV[:, SV_DBIAS:SV_DBIAS + 16], ALU.add),
              waits=[sv_ld], inc="sDVE")
        S.add("dve", f_ts(SV[:, SV_SUP:SV_SUP + 16], SV[:, SV_SU_UG:SV_SU_UG + 16],
                          1.0 / np.sqrt(D), None, ALU.mult), inc="sDVE")
        S.add("dve", f_tt(SV[:, SV_WSU2:SV_WSU2 + 64], SV[:, SV_WS_UP:SV_WS_UP + 64],
                          SV[:, SV_SU_DP:SV_SU_DP + 64], ALU.mult), inc="sDVE")
        S.add("dve", f_ts(SV[:, SV_WSU2:SV_WSU2 + 64], SV[:, SV_WSU2:SV_WSU2 + 64],
                          1.0 / np.sqrt(I), None, ALU.mult), inc="sDVE")
        sv_pre = S.n("sDVE")

        slot_free = [[("sW0", 0)], [("sW1", 0)], [("sW2", 0)], [("sW3", 0)]]
        next_slot = [0]

        def stream_block(dram_ap, two=True):
            s = next_slot[0]
            if two and s % 2 == 1:
                s = (s + 1) % 4
            k = dram_ap.shape[0] // P
            width = k * dram_ap.shape[1]
            assert width in (4096, 8192)
            out_ap = WR[:, s * 4096:s * 4096 + width].rearrange("p (k c) -> p k c", k=k)
            in_ap = dram_ap.rearrange("(k p) c -> p k c", p=P)
            w = list(slot_free[s])
            if two:
                w += slot_free[(s + 1) % 4]
            sem = f"sW{s}"
            n = dma("pool", out_ap, in_ap, sem, waits=w)
            next_slot[0] = (s + (2 if two else 1)) % 4
            return s, (sem, n)

        def mark_slot(slot, sem, cnt, two=True):
            slot_free[slot] = [(sem, cnt)]
            if two:
                slot_free[(slot + 1) % 4] = [(sem, cnt)]

        def wtile(slot, k, cwidth, c0):
            base = slot * 4096 + k * cwidth + c0
            return WR[:, base:base + P]

        bank_free = [("sPE", 0)] * 4
        bank_i = [0]

        def next_bank():
            b = bank_i[0]
            bank_i[0] = (b + 1) % 4
            return b

        ws_ready = [("sDVE", 0), ("sDVE", 0)]
        for sb in range(4):
            mbs = (2 * sb, 2 * sb + 1)
            blks = [stream_block(w_ind_d[mb]) for mb in mbs]
            ev_mb = {mb: [] for mb in mbs}
            for func, boff, half in ((AF.Silu, SV_B_IN, 0), (AF.Sigmoid, SV_BIAS2, 1)):
                for bi, mb in enumerate(mbs):
                    slot, ld = blks[bi]
                    base = R0 if mb % 2 == 0 else R1
                    for mt in range(2):
                        g = 2 * mb + mt
                        ocol = (base + 2048 + mt * T) if half == 0 else (base + mt * T)
                        for n in range(2):
                            b = next_bank()
                            pairs = [(wtile(slot, k, 512, half * 256 + mt * P),
                                      XBs(k, n * 512, 512)) for k in range(NT)]
                            mm = S.add("pe", f_mmgroup(pb[b][:, :], pairs),
                                       waits=[ld, xb_ld, bank_free[b]], inc="sPE")
                            ev = S.add("act", f_act(arf(ocol + n * 512, 512),
                                                    pb[b][:, :], func,
                                                    bias=svc(boff, g)),
                                       waits=[("sPE", mm), sv_ld,
                                              ws_ready[mb % 2]], inc="sACT")
                            bank_free[b] = ("sACT", ev)
                            ev_mb[mb].append(ev)
            for bi, mb in enumerate(mbs):
                mark_slot(blks[bi][0], "sACT", max(ev_mb[mb]))

            for mb in mbs:
                base = R0 if mb % 2 == 0 else R1
                dcol = lambda mt, base=base: base + mt * T
                icol = lambda mt, base=base: base + 2048 + mt * T
                ev_all = max(ev_mb[mb])
                for mt in range(2):
                    g = 2 * mb + mt
                    for n in range(2):
                        scr = R2 + 1024
                        S.add("dve", f_ts(arf(scr, 512),
                                          arf(dcol(mt) + n * 512, 512),
                                          -1.0, 1.0, ALU.mult, ALU.add),
                              waits=[("sACT", ev_all)], inc="sDVE")
                        S.add("dve", f_tt(arf(icol(mt) + n * 512, 512),
                                          arf(icol(mt) + n * 512, 512),
                                          arf(scr, 512), ALU.mult), inc="sDVE")
                    S.add("dve", f_scan(Y0s(g), arf(dcol(mt), T),
                                        arf(icol(mt), T), 0.0, ALU.mult, ALU.add),
                          inc="sDVE")
                    S.add("dve", f_scan(AGs(g), arf(dcol(mt), T),
                                        arf(dcol(mt), T), 1.0, ALU.mult, ALU.bypass),
                          inc="sDVE")
                    S.add("dve", f_copy(svc(SV_SFC, g),
                                        arb(Y0_OFF + g * T + T - 1, 1)), inc="sDVE")
                ws_ready[mb % 2] = ("sDVE", S.n("sDVE"))

        scans_all = S.n("sDVE")

        mk = S.add("dve", f_tt(SV[:, SV_SFC:SV_SFC + 16], SV[:, SV_SFC:SV_SFC + 16],
                               SV[:, SV_ME:SV_ME + 16], ALU.mult),
                   waits=[("sDVE", scans_all), sv_ld], inc="sDVE")
        dma("sync", ccin_d.ap().rearrange("(p g) -> p g", p=P),
            SV[:, SV_SFC:SV_SFC + 16], "sCI", waits=[("sDVE", mk)])
        ccin_st = S.n("sCI")

        gate_blocks = [stream_block(w_gat_d[0])]

        gate_free = [("sDVE", 0), ("sDVE", 0)]
        AGc = lambda k, n: arb(AG_OFF + k * T + n * 512, 512)
        Y0n = lambda k, n: arb(Y0_OFF + k * T + n * 512, 512)
        s0_ld = [0]
        for mb in range(8):
            if mb % 2 == 0 and 1 <= mb // 2 <= 3:
                gate_blocks.append(stream_block(w_gat_d[mb // 2]))
            if mb == 6:
                S.add("pool", lambda e: e.collective_compute(
                    "AllReduce", ALU.add,
                    replica_groups=[[0, 1], [2, 3], [4, 5], [6, 7]],
                    ins=[ccin_d.ap()], outs=[ccout_d.ap()]),
                    waits=[("sCI", ccin_st)], inc="sCC")
                dma("sync", SV[:, SV_S0:SV_S0 + 16],
                    ccout_d.ap().rearrange("(p g) -> p g", p=P), "sS0",
                    waits=[("sCC", 1)])
                s0_ld[0] = S.n("sS0")
            s_ga, ld_ga = gate_blocks[mb // 2]
            for mt in range(2):
                g = 2 * mb + mt
                for n in range(2):
                    gcol = R2 + n * 512
                    b = next_bank()
                    pairs = [(wtile(s_ga, k, 512, (mb % 2) * 256 + mt * P),
                              XBs(k, n * 512, 512)) for k in range(NT)]
                    mm = S.add("pe", f_mmgroup(pb[b][:, :], pairs),
                               waits=[ld_ga, xb_ld, bank_free[b]], inc="sPE")
                    ev = S.add("act", f_act(arf(gcol, 512), pb[b][:, :],
                                            AF.Sigmoid, bias=svc(SV_B_GAT, g)),
                               waits=[("sPE", mm), gate_free[n]], inc="sACT")
                    bank_free[b] = ("sACT", ev)
                    S.add("dve", f_tt(Y0n(g, n), Y0n(g, n), arf(gcol, 512),
                                      ALU.mult),
                          waits=[("sACT", ev)], inc="sDVE")
                    ag = S.add("dve", f_tt(AGc(g, n), AGc(g, n), arf(gcol, 512),
                                           ALU.mult), inc="sDVE")
                    gate_free[n] = ("sDVE", ag)
            if mb % 2 == 1:
                mark_slot(s_ga, "sDVE", S.n("sDVE"))

        S.add("dve", f_tt(SV[:, SV_S0:SV_S0 + 16], SV[:, SV_S0:SV_S0 + 16],
                          SV[:, SV_MO:SV_MO + 16], ALU.mult),
              waits=[("sS0", s0_ld[0])], inc="sDVE")
        corr_scr = ARF[:, R2 + 1024:R2 + 1024 + 512].bitcast(BF16)
        for g in range(NT):
            S.add("dve", f_ts(corr_scr, AGs(g), svc(SV_S0, g), None, ALU.mult),
                  inc="sDVE")
            S.add("dve", f_tt(Y0s(g), Y0s(g), corr_scr, ALU.add), inc="sDVE")
        y_done = S.n("sDVE")

        def ln_rows_and_bcast(last_vmm, last_sqmm, extra_dve=0, scratch=None):
            scr = pS0[0:1, :] if scratch is None else scratch
            r1 = S.add("dve", f_ts(STATR[0:1, 0:512], pS0[0:1, :], 1.0 / D, None,
                                   ALU.mult), waits=[("sPE", last_vmm)], inc="sDVE")
            S.add("dve", f_ts(STATR[0:1, 512:1024], pS1[0:1, :], 1.0 / D, None,
                              ALU.mult), waits=[("sPE", last_sqmm)], inc="sDVE")
            S.add("dve", f_tt(scr, STATR[0:1, 0:512],
                              STATR[0:1, 0:512], ALU.mult), inc="sDVE")
            r4 = S.add("dve", f_tt(STATR[0:1, 512:1024], STATR[0:1, 512:1024],
                                   scr, ALU.subtract), inc="sDVE")
            S.add("act", f_act(scr, STATR[0:1, 512:1024],
                               AF.Ln, bias=SV[0:1, SV_CONST:SV_CONST + 1]),
                  waits=[("sDVE", r4)], inc="sACT")
            r6 = S.add("act", f_act(STATR[0:1, 512:1024], scr, AF.Exp,
                                    scale=SV[0:1, SV_CONST + 1:SV_CONST + 2]),
                       inc="sACT")
            r8 = S.add("dve", f_stt(STATR[0:1, 0:512], STATR[0:1, 0:512], -1.0,
                                    STATR[0:1, 512:1024], ALU.mult, ALU.mult),
                       waits=[("sACT", r6)], inc="sDVE")
            bw = [("sACT", r6), ("sDVE", max(ones_done, extra_dve, r8))]
            bR = S.add("pe", f_mm1(pR[:, :], ONES1[0:1, :], STATR[0:1, 512:1024]),
                       waits=bw, inc="sPE")
            bM = S.add("pe", f_mm1(pM[:, :], ONES1[0:1, :], STATR[0:1, 0:512]),
                       waits=[("sDVE", r8)], inc="sPE")
            return max(bR, bM)

        xr_free = [("sDVE", y_done), ("sDVE", y_done)]
        Vbase = [R0, R1]
        stat_after = [0, 0]
        hb_marks = [[], []]
        x1_done = [0, 0]

        def emit_cpre_step(c, m, sub):
            if m % 4 == 0:
                sub["blk"] = stream_block(w_out_d[m // 4])
            s_o, ld_o = sub["blk"]
            xi = m % 2
            xr = R2 + xi * 512
            xl = dma("sync", arf(xr, 512),
                     xT_d[m * P:(m + 1) * P, c * CH:(c + 1) * CH],
                     f"sXR{xi}", waits=[xr_free[xi]])
            b = next_bank()
            pairs = [(wtile(s_o, k, 512, (m % 4) * P), Y0c(k, c)) for k in range(NT)]
            mm = S.add("pe", f_mmgroup(pb[b][:, :], pairs),
                       waits=[ld_o, bank_free[b], ("sDVE", y_done)], inc="sPE")
            vm = S.add("dve", f_stt(arf(Vbase[c] + m * 512, 512), pb[b][:, :],
                                    svc(SV_B_OUT, m), arf(xr, 512),
                                    ALU.add, ALU.add),
                       waits=[("sPE", mm), (f"sXR{xi}", xl)], inc="sDVE")
            bank_free[b] = ("sDVE", vm)
            xr_free[xi] = ("sDVE", vm)
            sub["v_marks"].append(vm)
            stw = [("sDVE", vm)]
            if stat_after[1 - c]:
                stw.append(("sPE", stat_after[1 - c]))
            S.add("act", f_act(STB(m), arf(Vbase[c] + m * 512, 512), AF.Copy),
                  waits=stw, inc="sACT")
            sq = S.add("act", f_act(STBq(m), arf(Vbase[c] + m * 512, 512), AF.Square),
                       inc="sACT")
            sub["stat_marks"].append(sq)
            if m % 4 == 3:
                mark_slot(s_o, "sPE", mm)

        def emit_cstats(c, sub, extra_dve=0):
            ssum = S.add("pe", f_mmgroup(pS0[0:1, :],
                                         [(ONESB[:, 0:1], STB(k)) for k in range(NT)]),
                         waits=[("sACT", max(sub["stat_marks"])),
                                ("sDVE", ones_done)], inc="sPE")
            ssq = S.add("pe", f_mmgroup(pS1[0:1, :],
                                        [(ONESB[:, 0:1], STBq(k)) for k in range(NT)]),
                        inc="sPE")
            stat_after[c] = ssq
            return ln_rows_and_bcast(ssum, ssq, extra_dve=extra_dve)

        def bfly_pair_cb(la, lb, sp):
            S.add("dve", f_tt(arb(sp, CH), arb(la, CH), arb(lb, CH), ALU.subtract),
                  inc="sDVE")
            S.add("dve", f_tt(arb(la, CH), arb(la, CH), arb(lb, CH), ALU.add),
                  inc="sDVE")

        def h128_pair(la, lb, marks):
            dv = S.n("sDVE")
            be = next_bank()
            S.add("pe", f_mm1(pb[be][:, :], H128[:, :], arb(la, CH),
                              start=True, stop=False),
                  waits=[("sDVE", dv), bank_free[be], sv_ld], inc="sPE")
            S.add("pe", f_mm1(pb[be][:, :], H128[:, :], arb(lb, CH),
                              start=False, stop=True), inc="sPE")
            bo = next_bank()
            S.add("pe", f_mm1(pb[bo][:, :], H128[:, :], arb(la, CH),
                              start=True, stop=False),
                  waits=[bank_free[bo]], inc="sPE")
            mmo = S.add("pe", f_mm1(pb[bo][:, :], H128N[:, :], arb(lb, CH),
                                    start=False, stop=True), inc="sPE")
            eve = S.add("act", f_act(arb(la, CH), pb[be][:, :], AF.Copy),
                        waits=[("sPE", mmo)], inc="sACT")
            evo = S.add("act", f_act(arb(lb, CH), pb[bo][:, :], AF.Copy),
                        inc="sACT")
            bank_free[be] = ("sACT", eve)
            bank_free[bo] = ("sACT", evo)
            marks.append(evo)

        hb_loc = [None, None]

        def make_bfly1(c, spare):
            locs = {g: XB_OFF + g * T + c * CH for g in range(NT)}

            def final_cb(p, la, lb):
                h128_pair(la, lb, hb_marks[c])
            bf = Bfly(NT, locs, spare, bfly_pair_cb, final_cb)
            hb_loc[c] = bf.loc
            return bf

        def emit_dpost_step(c, m, sub, bcast):
            tcol = R2 + 1024
            vt = arf(Vbase[c] + m * 512, 512)
            S.add("dve", f_tt(arf(tcol, 512), vt, pR[:, :], ALU.mult),
                  waits=[("sPE", bcast),
                         ("sACT", max(sub["stat_marks"]))], inc="sDVE")
            t2 = S.add("dve", f_tt(vt, arf(tcol, 512), pM[:, :], ALU.add),
                       inc="sDVE")
            S.add("dve", f_ts(HBs(m, c), vt, svc(SV_SUP, m), None, ALU.mult),
                  inc="sDVE")
            sub["bfly"].tile_arrived(m)
            sub["nsteps"] = sub.get("nsteps", 0) + 1
            if sub["nsteps"] == NT:
                x1_done[c] = S.n("sDVE")

        DPERM = [v for p in zip(range(NT // 2), range(NT // 2, NT)) for v in p]

        sub0 = {"v_marks": [], "stat_marks": [], "bfly": make_bfly1(0, SPARE_OFF)}
        for m in range(NT):
            emit_cpre_step(0, m, sub0)
        bcast0 = emit_cstats(0, sub0)

        sub1 = {"v_marks": [], "stat_marks": [],
                "bfly": make_bfly1(1, SPARE_OFF + CH)}
        for m in range(NT):
            emit_cpre_step(1, m, sub1)
            emit_dpost_step(0, DPERM[m], sub0, bcast0)
        bcast1 = emit_cstats(1, sub1, extra_dve=x1_done[0])

        d1_state = {"m": 0}

        def pump_d1():
            m = d1_state["m"]
            if m < NT:
                emit_dpost_step(1, DPERM[m], sub1, bcast1)
                d1_state["m"] = m + 1
            elif m == NT:
                d1_state["store"] = dma(
                    "sync", x1h_d.ap().rearrange("(m p) t -> p m t", p=P),
                    ARF[:, R1:R1 + 8192].rearrange("p (m t) -> p m t", m=NT),
                    "sX1S", waits=[("sDVE", x1_done[1])])
                d1_state["m"] = m + 1

        stg_free = [("sU0", 0), ("sU1", 0)]
        stg_idx = [0]

        def stg_ap(j):
            return ARF[:, R2 + 1024 + j * 256: R2 + 1024 + (j + 1) * 256]\
                .bitcast(BF16)

        vt_marks0 = []
        locs2 = {i: U2_OFF + i * CH for i in range(NI)}
        bfly2 = Bfly(NI, locs2, sub0["bfly"].stale_slot(), bfly_pair_cb,
                     lambda p, la, lb: h128_pair(la, lb, vt_marks0))
        gt_ring = [("sDVE", S.n("sDVE")), ("sDVE", S.n("sDVE"))]
        gi_ctr = [0]
        last_ug_mm = [0]

        def emit_ug_tile(c, tile, s_ug, ld_ug):
            hb_ok = max(hb_marks[c])
            bu = next_bank()
            pairs_u = [(wtile(s_ug, k, 512, (tile % 2) * P),
                        arb(hb_loc[c][k] + 0, CH)) for k in range(NT)]
            mmup = S.add("pe", f_mmgroup(pb[bu][:, :], pairs_u),
                         waits=[ld_ug, bank_free[bu], ("sACT", hb_ok)], inc="sPE")
            bg = next_bank()
            pairs_g = [(wtile(s_ug, k, 512, 256 + (tile % 2) * P),
                        arb(hb_loc[c][k] + 0, CH)) for k in range(NT)]
            mmgt = S.add("pe", f_mmgroup(pb[bg][:, :], pairs_g),
                         waits=[ld_ug, bank_free[bg], ("sACT", hb_ok)], inc="sPE")
            last_ug_mm[0] = mmgt
            gi = gi_ctr[0] % 2
            gi_ctr[0] += 1
            gcol = R2 + gi * 512
            sg = S.add("act", f_act(arf(gcol, 512), pb[bg][:, :], AF.Silu,
                                    scale=svc(SV_WS_GT, tile)),
                       waits=[("sPE", mmgt), gt_ring[gi], sv_ld], inc="sACT")
            if c == 0:
                u2 = S.add("dve", f_stt(U2s(tile), pb[bu][:, :], svc(SV_WSU2, tile),
                                        arf(gcol, 512), ALU.mult, ALU.mult),
                           waits=[("sPE", mmup), ("sACT", sg), ("sDVE", sv_pre)],
                           inc="sDVE")
            else:
                j = stg_idx[0] % 2
                stg_idx[0] += 1
                u2 = S.add("dve", f_stt(stg_ap(j), pb[bu][:, :], svc(SV_WSU2, tile),
                                        arf(gcol, 512), ALU.mult, ALU.mult),
                           waits=[("sPE", mmup), ("sACT", sg), ("sDVE", sv_pre),
                                  stg_free[j]], inc="sDVE")
                d = tile // 2
                sp = DPOS[d] if tile % 2 == 0 else 32 + DPOS[d]
                n = dma("sync", u2h_d[sp], stg_ap(j), f"sU{j}",
                        waits=[("sDVE", u2)])
                stg_free[j] = (f"sU{j}", n)
            bank_free[bu] = ("sDVE", u2)
            bank_free[bg] = ("sDVE", u2)
            gt_ring[gi] = ("sDVE", u2)
            if c == 0:
                bfly2.tile_arrived(tile)
            return u2

        border = [b for pr in zip(range(NI // 4), range(NI // 4, NI // 2))
                  for b in pr]
        DORDER = [d for pr in zip(range(16), range(16, 32)) for d in pr]
        DPOS = {d: i for i, d in enumerate(DORDER)}
        for pos in range(32):
            b = border[pos]
            if pos == 8:
                while d1_state["m"] <= NT:
                    pump_d1()
            s_ug, ld_ug = stream_block(w_ug_d[b])
            for tile in (2 * b, 2 * b + 1):
                emit_ug_tile(0, tile, s_ug, ld_ug)
                if 8 <= pos < 30:
                    emit_ug_tile(1, tile, s_ug, ld_ug)
            mark_slot(s_ug, "sDVE", S.n("sDVE"))
            if pos < 8:
                pump_d1()
                pump_d1()
        for b in border[:8] + border[30:]:
            s_ug, ld_ug = stream_block(w_ug_d[b])
            for tile in (2 * b, 2 * b + 1):
                emit_ug_tile(1, tile, s_ug, ld_ug)
            mark_slot(s_ug, "sDVE", S.n("sDVE"))
        vt_ok0 = max(vt_marks0)
        init_spare0 = sub0["bfly"].stale_slot()
        reloc_mark = 0
        for t, lo in list(bfly2.loc.items()):
            if lo == init_spare0:
                stale = bfly2.stale_slot()
                reloc_mark = S.add(
                    "dve", f_copy(arb(stale, CH), arb(lo, CH)),
                    waits=[("sACT", vt_ok0)], inc="sDVE")
                bfly2.loc[t] = stale
        u2_last_mark = S.n("sDVE")

        def emit_down_ln2(c, loc_of, vt_ok, x1_ready, per_m_pre=None,
                          f_extra=(), mm_extra=()):
            f_free = ("sDVE", 0)
            sq_free = [("sPE", 0), ("sPE", 0)]
            v2_marks = []
            last_vmm = last_sqmm = 0
            down_last = [0]
            for m in range(NT):
                s_dn, ld_dn = stream_block(w_dn_d[m])
                b = next_bank()
                pairs = [(wtile(s_dn, k, 128, 0), arb(loc_of[k], CH))
                         for k in range(NI)]
                mm = S.add("pe", f_mmgroup(pb[b][:, :], pairs),
                           waits=[ld_dn, bank_free[b], ("sACT", vt_ok)]
                                 + list(mm_extra), inc="sPE")
                down_last[0] = mm
                extra = list(per_m_pre(m)) if per_m_pre is not None else []
                fi = m % 2
                fcol = R2 + 1024
                fw = [("sPE", mm), f_free]
                if m < 1:
                    fw.append(("sDVE", S.n("sDVE")))
                    fw += list(f_extra)
                fe = S.add("act", f_act(arf(fcol, 512), pb[b][:, :], AF.Copy,
                                        scale=svc(SV_WS_DP, m)), waits=fw, inc="sACT")
                bank_free[b] = ("sACT", fe)
                v2w = [("sACT", fe), x1_ready] + extra
                src = R0 if c == 0 else R1
                v2 = S.add("dve", f_tt(arf(R1 + m * 512, 512),
                                       arf(src + m * 512, 512),
                                       arf(fcol, 512), ALU.add),
                           waits=v2w, inc="sDVE")
                f_free = ("sDVE", v2)
                v2_marks.append(v2)
                last_vmm = S.add("pe", f_mm1(pS0[0:1, :], ONESF[:, 0:1],
                                             arf(R1 + m * 512, 512),
                                             start=(m == 0), stop=(m == NT - 1)),
                                 waits=[("sDVE", v2), ("sDVE", ones_done)], inc="sPE")
                sqw = [("sDVE", v2), sq_free[fi]]
                if m < 2:
                    sqw.append(("sDVE", u2_last_mark))
                sq = S.add("act", f_act(arf(R2 + fi * 512, 512),
                                        arf(R1 + m * 512, 512), AF.Square),
                           waits=sqw, inc="sACT")
                last_sqmm = S.add("pe", f_mm1(pS1[0:1, :], ONESF[:, 0:1],
                                              arf(R2 + fi * 512, 512),
                                              start=(m == 0), stop=(m == NT - 1)),
                                  waits=[("sACT", sq)], inc="sPE")
                sq_free[fi] = ("sPE", last_sqmm)
                mark_slot(s_dn, "sPE", mm)
            return down_last[0], v2_marks, last_vmm, last_sqmm

        def emit_norm_step(c, m, bcast, v2_all, store_eng, obase=R1):
            t1 = S.add("dve", f_tt(arf(obase + m * 512, 512),
                                   arf(R1 + m * 512, 512),
                                   pR[:, :], ALU.mult),
                       waits=[("sPE", bcast), ("sDVE", v2_all)], inc="sDVE")
            t2 = S.add("dve", f_tt(arf(obase + m * 512, 512),
                                   arf(obase + m * 512, 512),
                                   pM[:, :], ALU.add), inc="sDVE")
            stn = dma(store_eng, out_d[m * P:(m + 1) * P, c * CH:(c + 1) * CH],
                      arf(obase + m * 512, 512), f"sST{m % 2}",
                      waits=[("sDVE", t2)])
            return t1, t2, stn

        ev_locs = {d: XB_OFF + DPOS[d] * 512 for d in range(32)}
        od_locs = {d: U2_OFF + DPOS[d] * 512 for d in range(32)}
        ev_final = {}
        vt_marks1 = []
        loc1_final = {}

        def ev_done(d, loc, _):
            ev_final[d] = loc

        def od_done(d, loc, _):
            la = ev_final[d]
            h128_pair(la, loc, vt_marks1)
            loc1_final[2 * d] = la
            loc1_final[2 * d + 1] = loc
        bfe = Bfly(32, ev_locs, SPARE_OFF, bfly_pair_cb, ev_done, absorb_h1=False)
        bfo = Bfly(32, od_locs, SPARE_OFF + 512, bfly_pair_cb, od_done,
                   absorb_h1=False)

        ev_dmas = []
        for g in range(4):
            n = dma("sync",
                    ARB[:, XB_OFF + g * 8 * 512: XB_OFF + (g + 1) * 8 * 512]
                    .rearrange("p (i t) -> p i t", i=8),
                    u2h_d[g * 8:(g + 1) * 8].rearrange("i p t -> p i t"),
                    f"sU{g % 2}",
                    waits=[("sPE", last_ug_mm[0]),
                           ("sU0", S.n("sU0")), ("sU1", S.n("sU1"))])
            ev_dmas.append((f"sU{g % 2}", n))
        ev_state = {"g": 0}

        def pump_even(m):
            if m % 4 == 0 and ev_state["g"] < 4:
                g = ev_state["g"]
                ev_state["g"] = g + 1
                S.add("dve", lambda e: e.nop(), waits=[ev_dmas[g]], inc="sDVE")
                for i in range(g * 8, (g + 1) * 8):
                    bfe.tile_arrived(DORDER[i])
            return []

        x1_ready0 = ("sX1S", d1_state["store"])
        dn0_last, v2m0, lvmm0, lsqm0 = emit_down_ln2(
            0, bfly2.loc, vt_ok0, x1_ready0, per_m_pre=pump_even,
            f_extra=[stg_free[0], stg_free[1]], mm_extra=[("sDVE", reloc_mark)])

        od_dmas = []
        for g in range(4):
            n = dma("pool",
                    ARB[:, U2_OFF + g * 8 * 512: U2_OFF + (g + 1) * 8 * 512]
                    .rearrange("p (i t) -> p i t", i=8),
                    u2h_d[32 + g * 8: 32 + (g + 1) * 8].rearrange("i p t -> p i t"),
                    f"sU{g % 2}",
                    waits=[("sPE", dn0_last),
                           (f"sU{g % 2}", S.n(f"sU{g % 2}"))])
            S.add("dve", lambda e: e.nop(), waits=[(f"sU{g % 2}", n)], inc="sDVE")
            for i in range(g * 8, (g + 1) * 8):
                bfo.tile_arrived(DORDER[i])
        vt_ok1 = max(vt_marks1)

        bcast0 = ln_rows_and_bcast(lvmm0, lsqm0, extra_dve=x1_done[1],
                                   scratch=pR[0:1, :])
        x1slab = {}

        def emit_x1_slab(m):
            t1, t2, _ = emit_norm_step(0, m, bcast0, max(v2m0), "sync", obase=R0)
            n = dma("sync", arf(R1 + m * 512, 512),
                    x1h_d[m * P:(m + 1) * P, :], "sX1",
                    waits=[("sDVE", t1), ("sX1", S.n("sX1")),
                           ("sX1S", d1_state["store"])])
            x1slab[m] = ("sX1", n)

        emit_x1_slab(0)
        emit_x1_slab(1)

        def pre_p1(m):
            if m + 2 < NT:
                emit_x1_slab(m + 2)
            return [x1slab[m]]

        dn1_last, v2m1, lvmm1, lsqm1 = emit_down_ln2(
            1, loc1_final, vt_ok1, ("sX1", 0), per_m_pre=pre_p1)

        bcast1 = ln_rows_and_bcast(lvmm1, lsqm1)
        for m in range(NT):
            emit_norm_step(1, m, bcast1, max(v2m1), "sync")

        S.add("sync", lambda e: e.nop(),
              waits=[("sST0", S.n("sST0")), ("sST1", S.n("sST1"))])

        @block.sync
        def _(sy):
            S.replay("sync", sy, sems)

        @block.gpsimd
        def _(g):
            S.replay("pool", g, sems)

        @block.tensor
        def _(t):
            S.replay("pe", t, sems)

        @block.scalar
        def _(sc):
            S.replay("act", sc, sems)

        @block.vector
        def _(v):
            S.replay("dve", v, sems)

    return nc


def _vec16(v):
    return np.ascontiguousarray(v.reshape(16, P).T)


def _vec64(v):
    return np.ascontiguousarray(v.reshape(64, P).T)


def _blocks(W, nblk, width):
    K, M = W.shape
    assert M == nblk * width
    return np.ascontiguousarray(W.reshape(K, nblk, width).transpose(1, 0, 2))


_NC_CACHE = {}


def kernel(L0_attn_x, L0_attn_W_in, L0_attn_b_in, L0_attn_W_decay, L0_attn_b_decay,
           L0_attn_W_gate, L0_attn_b_gate, L0_attn_W_out, L0_attn_b_out,
           L0_attn_decay_bias, L0_ffn_SU_ug, L0_ffn_SU_dp, L0_ffn_Wscale_ug,
           L0_ffn_Wscale_dp, L0_ffn_W_up, L0_ffn_W_gate, L0_ffn_W_down):
    x = np.asarray(L0_attn_x, np.float32)
    f = lambda a: np.asarray(a, np.float32)

    bi_ = _blocks(f(L0_attn_W_in), 8, 256)
    bd_ = _blocks(f(L0_attn_W_decay), 8, 256)
    w_ind = np.ascontiguousarray(np.concatenate([bi_, bd_], axis=2))
    w_gat = _blocks(f(L0_attn_W_gate), 4, 512)
    w_out = _blocks(f(L0_attn_W_out), 4, 512)
    bu2 = _blocks(np.ascontiguousarray(f(L0_ffn_W_up).T), 32, 256)
    bg2 = _blocks(np.ascontiguousarray(f(L0_ffn_W_gate).T), 32, 256)
    w_ug = np.ascontiguousarray(np.concatenate([bu2, bg2], axis=2))
    w_dn = _blocks(np.ascontiguousarray(f(L0_ffn_W_down).T), 16, 128)

    h128 = hadamard(P).astype(ml_dtypes.bfloat16)
    h128n = (-hadamard(P)).astype(ml_dtypes.bfloat16)

    sv_base = np.zeros((P, SV_W), np.float32)
    sv_base[:, SV_B_IN:SV_B_IN + 16] = _vec16(f(L0_attn_b_in))
    sv_base[:, SV_B_DEC:SV_B_DEC + 16] = _vec16(f(L0_attn_b_decay))
    sv_base[:, SV_DBIAS:SV_DBIAS + 16] = _vec16(f(L0_attn_decay_bias))
    sv_base[:, SV_B_GAT:SV_B_GAT + 16] = _vec16(f(L0_attn_b_gate))
    sv_base[:, SV_B_OUT:SV_B_OUT + 16] = _vec16(f(L0_attn_b_out))
    sv_base[:, SV_SU_UG:SV_SU_UG + 16] = _vec16(f(L0_ffn_SU_ug))
    sv_base[:, SV_WS_DP:SV_WS_DP + 16] = _vec16(f(L0_ffn_Wscale_dp))
    sv_base[:, SV_WS_UP:SV_WS_UP + 64] = _vec64(f(L0_ffn_Wscale_ug)[:I])
    sv_base[:, SV_WS_GT:SV_WS_GT + 64] = _vec64(f(L0_ffn_Wscale_ug)[I:])
    sv_base[:, SV_SU_DP:SV_SU_DP + 64] = _vec64(f(L0_ffn_SU_dp))
    sv_base[:, SV_CONST] = EPS
    sv_base[:, SV_CONST + 1] = -0.5

    if "prog" not in _NC_CACHE:
        _NC_CACHE["prog"] = build_program()
    nc = _NC_CACHE["prog"]

    in_maps = []
    for c in range(8):
        b, h = c // 2, c % 2
        xT = np.ascontiguousarray(x[b, h * T:(h + 1) * T, :].T)
        sv = sv_base.copy()
        sv[:, SV_ME:SV_ME + 16] = 1.0 if h == 0 else 0.0
        sv[:, SV_MO:SV_MO + 16] = 1.0 if h == 1 else 0.0
        in_maps.append({
            "xT": xT, "w_ind": w_ind, "w_gat": w_gat, "w_out": w_out,
            "w_ug": w_ug, "w_dn": w_dn, "sv": sv, "h128": h128,
            "h128n": h128n,
        })

    res = run_bass_kernel_spmd(nc, in_maps, core_ids=list(range(8)))

    out = np.empty((4, 2048, D), np.float32)
    for c in range(8):
        b, h = c // 2, c % 2
        out[b, h * T:(h + 1) * T, :] = res.results[c]["out"].T
    return out


if __name__ == "__main__":
    import reference
    ins = reference.setup_inputs()
    got = kernel(**{k: np.asarray(v) for k, v in ins.items()})
    exp = np.asarray(reference.reference(**ins))
    err = np.abs(got - exp)
    rel = np.sqrt(((got - exp) ** 2).mean() / (exp ** 2).mean())
    print("maxabs", err.max(), "relL2", rel)


# version 19
# speedup vs baseline: 1.1059x; 1.1059x over previous
import numpy as np
import ml_dtypes
from contextlib import ExitStack

import concourse.bass as bass
import concourse.mybir as mybir
from concourse.bass_utils import run_bass_kernel_spmd

F32 = mybir.dt.float32
BF16 = mybir.dt.bfloat16
AF = mybir.ActivationFunctionType
ALU = mybir.AluOpType

P = 128
D = 2048
I = 8192
T = 1024
NT = D // P
NI = I // P
CH = 512
NCH = T // CH
EPS = 1e-5

SV_B_IN, SV_B_DEC, SV_DBIAS, SV_B_GAT, SV_B_OUT = 0, 16, 32, 48, 64
SV_SU_UG, SV_WS_DP, SV_ME, SV_MO = 80, 96, 112, 128
SV_WS_UP, SV_WS_GT, SV_SU_DP = 144, 208, 272
SV_BIAS2, SV_SUP, SV_WSU2, SV_SFC, SV_S0 = 336, 352, 368, 432, 448
SV_CONST = 464
SV_W = 468

R0, R1, R2 = 0, 8192, 16384
ARF_W = 17920
XB_OFF, Y0_OFF, AG_OFF = 0, 16384, 32768
U2_OFF = 16384
SPARE_OFF = 49152
ARB_W = 50176

ENGS = ("sync", "pool", "pe", "act", "dve")


class Sched:
    def __init__(self):
        self.items = {e: [] for e in ENGS}
        self.cnt = {}

    def n(self, sem):
        return self.cnt.get(sem, 0)

    def add(self, eng, fn, waits=(), inc=None, inc_by=1):
        w = {}
        for s, v in waits:
            if v > 0:
                w[s] = max(w.get(s, 0), v)
        after = None
        if inc is not None:
            self.cnt[inc] = self.cnt.get(inc, 0) + inc_by
            after = self.cnt[inc]
        self.items[eng].append((fn, sorted(w.items()), inc, inc_by))
        return after

    def replay(self, eng_name, eng_obj, sems):
        for fn, waits, inc, inc_by in self.items[eng_name]:
            for s, v in waits:
                eng_obj.wait_ge(sems[s], v)
            r = fn(eng_obj)
            if inc is not None:
                if r is None:
                    r = eng_obj.nop()
                r.then_inc(sems[inc], inc_by)


def f_tt(o, a, b, op):
    return lambda e, o=o, a=a, b=b, op=op: e.tensor_tensor(out=o, in0=a, in1=b, op=op)


def f_ts(o, a, s1, s2, op0, op1=None):
    if op1 is None:
        return lambda e, o=o, a=a, s1=s1: e.tensor_scalar(
            out=o, in0=a, scalar1=s1, scalar2=None, op0=ALU.mult)
    return lambda e, o=o, a=a, s1=s1, s2=s2, op0=op0, op1=op1: e.tensor_scalar(
        out=o, in0=a, scalar1=s1, scalar2=s2, op0=op0, op1=op1)


def f_stt(o, a, sc, b, op0, op1):
    return lambda e, o=o, a=a, sc=sc, b=b, op0=op0, op1=op1: e.scalar_tensor_tensor(
        out=o, in0=a, scalar=sc, in1=b, op0=op0, op1=op1)


def f_act(o, a, func, bias=None, scale=None):
    def run(e, o=o, a=a, func=func, bias=bias, scale=scale):
        kw = {}
        if bias is not None:
            kw["bias"] = bias
        if scale is not None:
            kw["scale"] = scale
        return e.activation(out=o, in_=a, func=func, **kw)
    return run


def f_scan(o, d0, d1, init, op0, op1):
    return lambda e, o=o, d0=d0, d1=d1, init=init, op0=op0, op1=op1: \
        e.tensor_tensor_scan(out=o, data0=d0, data1=d1, initial=init, op0=op0, op1=op1)


def f_copy(o, a):
    return lambda e, o=o, a=a: e.tensor_copy(o, a)


def f_mmgroup(out_ap, pairs):
    def run(e, out_ap=out_ap, pairs=pairs):
        last = len(pairs) - 1
        r = None
        for j, (l, rh) in enumerate(pairs):
            r = e.matmul(out_ap, l, rh, start=(j == 0), stop=(j == last))
        return r
    return run


def f_mmgroup_flags(out_ap, pairs, start, stop):
    def run(e, out_ap=out_ap, pairs=pairs, start=start, stop=stop):
        last = len(pairs) - 1
        r = None
        for j, (l, rh) in enumerate(pairs):
            r = e.matmul(out_ap, l, rh, start=(start and j == 0),
                         stop=(stop and j == last))
        return r
    return run


def f_mm1(out_ap, l, rh, start=True, stop=True):
    return lambda e, o=out_ap, l=l, rh=rh, st=start, sp=stop: e.matmul(
        o, l, rh, start=st, stop=sp)


def f_memset(ap, val):
    return lambda e, ap=ap, val=val: e.memset(ap, val)


def hadamard(n):
    H = np.array([[1.0]], dtype=np.float64)
    while H.shape[0] < n:
        H = np.block([[H, H], [H, -H]])
    return H


class Bfly:

    def __init__(self, ntiles, locs, spare, pair_cb, final_cb, absorb_h1=True):
        self.n = ntiles
        self.loc = dict(locs)
        self.spare = spare
        self.pair_cb = pair_cb
        self.final_cb = final_cb
        self.absorb_h1 = absorb_h1
        self.done = set()
        self.arrived = set()
        self.final_tiles = set()
        self.fired = set()

    def _dep_ok(self, j, h):
        if 2 * h >= self.n:
            return j in self.arrived and (j + h) in self.arrived
        d0 = (j & ~(2 * h), 2 * h)
        d1 = ((j + h) & ~(2 * h), 2 * h)
        return d0 in self.done and d1 in self.done

    def stale_slot(self):
        lv = self.n.bit_length() - (2 if self.absorb_h1 else 1)
        assert len(self.done) == (self.n // 2) * lv
        return self.spare

    def tile_arrived(self, t):
        self.arrived.add(t)
        hmin = 2 if self.absorb_h1 else 1
        prog = True
        while prog:
            prog = False
            h = self.n // 2
            while h >= hmin:
                for j in range(self.n):
                    if (j & h) == 0 and (j, h) not in self.done and self._dep_ok(j, h):
                        la, lb = self.loc[j], self.loc[j + h]
                        self.pair_cb(la, lb, self.spare)
                        self.loc[j + h] = self.spare
                        self.spare = lb
                        self.done.add((j, h))
                        if h == hmin:
                            if self.absorb_h1:
                                self.final_tiles.add(j)
                                self.final_tiles.add(j + 2)
                                for p in (j & ~1, (j + 2) & ~1):
                                    if (p in self.final_tiles
                                            and p + 1 in self.final_tiles
                                            and p not in self.fired):
                                        self.fired.add(p)
                                        self.final_cb(p, self.loc[p],
                                                      self.loc[p + 1])
                            else:
                                self.final_cb(j, self.loc[j], None)
                                self.final_cb(j + h, self.loc[j + h], None)
                        prog = True
                h //= 2


def build_program():
    nc = bass.Bass("TRN2", target_bir_lowering=False, debug=False)

    xT_d = nc.dram_tensor("xT", [D, T], F32, kind="ExternalInput")
    w_ind_d = nc.dram_tensor("w_ind", [8, D, 512], F32, kind="ExternalInput")
    w_gat_d = nc.dram_tensor("w_gat", [4, D, 512], F32, kind="ExternalInput")
    w_out_d = nc.dram_tensor("w_out", [4, D, 512], F32, kind="ExternalInput")
    w_ug_d = nc.dram_tensor("w_ug", [32, D, 512], F32, kind="ExternalInput")
    w_dn_d = nc.dram_tensor("w_dn", [16, I, 128], F32, kind="ExternalInput")
    sv_d = nc.dram_tensor("sv", [P, SV_W], F32, kind="ExternalInput")
    h128_d = nc.dram_tensor("h128", [P, P], BF16, kind="ExternalInput")
    h128n_d = nc.dram_tensor("h128n", [P, P], BF16, kind="ExternalInput")
    out_d = nc.dram_tensor("out", [D, T], F32, kind="ExternalOutput")
    x1h_d = nc.dram_tensor("x1h", [D, CH], F32)
    u2h_d = nc.dram_tensor("u2h", [NI, P, CH], BF16)
    wc_ug_d = nc.dram_tensor("wc_ug", [32, P, 8192], BF16)
    wc_dn_d = nc.dram_tensor("wc_dn", [16, P, 8192], BF16)
    ccin_d = nc.dram_tensor("ccin", [D], F32)
    ccout_d = nc.dram_tensor("ccout", [D], F32)

    S = Sched()

    with ExitStack() as ctx:
        sbuf = lambda n, sh, dt: ctx.enter_context(nc.sbuf_tensor(n, sh, dt))
        psumt = lambda n, sh: ctx.enter_context(nc.psum_tensor(n, sh, F32))
        mksem = lambda n: ctx.enter_context(nc.semaphore(n))

        ARB = sbuf("ARB", [P, ARB_W], BF16)
        ARF = sbuf("ARF", [P, ARF_W], F32)
        WR = sbuf("WR", [P, 16384], BF16)
        SV = sbuf("SV", [P, SV_W], F32)
        H128 = sbuf("H128", [P, P], BF16)
        H128N = sbuf("H128N", [P, P], BF16)
        ONESB = sbuf("ONESB", [P, 1], BF16)
        ONESF = sbuf("ONESF", [P, 1], F32)
        ONES1 = sbuf("ONES1", [1, P], F32)
        STATR = sbuf("STATR", [1, 1024], F32)

        pb = [psumt(f"pb{i}", [P, 512]) for i in range(4)]
        pR = psumt("pR", [P, 512])
        pM = psumt("pM", [P, 512])
        pS0 = psumt("pS0", [1, 512])
        pS1 = psumt("pS1", [1, 512])

        sems = {n: mksem(n) for n in
                ("sSP", "sXB", "sW0", "sW1", "sW2", "sW3", "sXR0", "sXR1",
                 "sX1", "sX1S", "sCI", "sS0", "sST0", "sST1",
                 "sU0", "sU1", "sCC", "sPE", "sACT", "sDVE")}
        block = ctx.enter_context(nc.Block())

        arb = lambda off, w: ARB[:, off:off + w]
        arf = lambda off, w: ARF[:, off:off + w]
        XBs = lambda k, a, w: arb(XB_OFF + k * T + a, w)
        HBs = lambda k, c: arb(XB_OFF + k * T + c * CH, CH)
        Y0s = lambda k: arb(Y0_OFF + k * T, T)
        Y0c = lambda k, c: arb(Y0_OFF + k * T + c * CH, CH)
        AGs = lambda k: arb(AG_OFF + k * T, T)
        U2s = lambda i: arb(U2_OFF + i * CH, CH)
        STB = lambda m: arb(AG_OFF + m * CH, CH)
        STBq = lambda m: arb(AG_OFF + 8192 + m * CH, CH)
        svc = lambda off, g: SV[:, off + g:off + g + 1]

        def dma(eng, out_ap, in_ap, sem, waits=()):
            fn = lambda e, o=out_ap, i=in_ap: e.dma_start(out=o, in_=i)
            return S.add(eng, fn, waits=waits, inc=sem, inc_by=16)

        dma("sync", SV[:, :], sv_d[:, :], "sSP")
        dma("sync", H128[:, :], h128_d[:, :], "sSP")
        dma("sync", H128N[:, :], h128n_d[:, :], "sSP")
        sv_ld = ("sSP", 48)

        S.add("dve", f_memset(ONESB[:, :], 1.0), inc="sDVE")
        S.add("dve", f_memset(ONESF[:, :], 1.0), inc="sDVE")
        S.add("dve", f_memset(ONES1[:, :], 1.0), inc="sDVE")
        ones_done = S.n("sDVE")

        def emit_xb_quarters():
            for q in range(4):
                dma("pool",
                    ARB[:, XB_OFF + q * 4 * T: XB_OFF + (q + 1) * 4 * T]
                    .rearrange("p (k t) -> p k t", k=4),
                    xT_d[q * 4 * P:(q + 1) * 4 * P, :]
                    .rearrange("(k p) t -> p k t", p=P), "sXB")
        xb_ld = ("sXB", 64)

        S.add("dve", f_tt(SV[:, SV_BIAS2:SV_BIAS2 + 16], SV[:, SV_B_DEC:SV_B_DEC + 16],
                          SV[:, SV_DBIAS:SV_DBIAS + 16], ALU.add),
              waits=[sv_ld], inc="sDVE")
        S.add("dve", f_ts(SV[:, SV_SUP:SV_SUP + 16], SV[:, SV_SU_UG:SV_SU_UG + 16],
                          1.0 / np.sqrt(D), None, ALU.mult), inc="sDVE")
        S.add("dve", f_tt(SV[:, SV_WSU2:SV_WSU2 + 64], SV[:, SV_WS_UP:SV_WS_UP + 64],
                          SV[:, SV_SU_DP:SV_SU_DP + 64], ALU.mult), inc="sDVE")
        S.add("dve", f_ts(SV[:, SV_WSU2:SV_WSU2 + 64], SV[:, SV_WSU2:SV_WSU2 + 64],
                          1.0 / np.sqrt(I), None, ALU.mult), inc="sDVE")
        sv_pre = S.n("sDVE")

        slot_free = [[("sW0", 0)], [("sW1", 0)], [("sW2", 0)], [("sW3", 0)]]
        next_slot = [0]

        def stream_block(dram_ap, two=True):
            s = next_slot[0]
            if two and s % 2 == 1:
                s = (s + 1) % 4
            k = dram_ap.shape[0] // P
            width = k * dram_ap.shape[1]
            assert width in (4096, 8192)
            out_ap = WR[:, s * 4096:s * 4096 + width].rearrange("p (k c) -> p k c", k=k)
            in_ap = dram_ap.rearrange("(k p) c -> p k c", p=P)
            w = list(slot_free[s])
            if two:
                w += slot_free[(s + 1) % 4]
            sem = f"sW{s}"
            n = dma("pool", out_ap, in_ap, sem, waits=w)
            next_slot[0] = (s + (2 if two else 1)) % 4
            return s, (sem, n)

        def mark_slot(slot, sem, cnt, two=True):
            slot_free[slot] = [(sem, cnt)]
            if two:
                slot_free[(slot + 1) % 4] = [(sem, cnt)]

        def wtile(slot, k, cwidth, c0):
            base = slot * 4096 + k * cwidth + c0
            return WR[:, base:base + P]

        bank_free = [("sPE", 0)] * 4
        bank_i = [0]

        def next_bank():
            b = bank_i[0]
            bank_i[0] = (b + 1) % 4
            return b

        ws_ready = [("sDVE", 0), ("sDVE", 0)]
        for sb in range(4):
            mbs = (2 * sb, 2 * sb + 1)
            if sb == 0:
                blks = [stream_block(w_ind_d[mbs[0]])]
                emit_xb_quarters()
                blks.append(stream_block(w_ind_d[mbs[1]]))
            else:
                blks = [stream_block(w_ind_d[mb]) for mb in mbs]
            ev_mb = {mb: [] for mb in mbs}
            for func, boff, half in ((AF.Silu, SV_B_IN, 0), (AF.Sigmoid, SV_BIAS2, 1)):
                for bi, mb in enumerate(mbs):
                    slot, ld = blks[bi]
                    base = R0 if mb % 2 == 0 else R1
                    for mt in range(2):
                        g = 2 * mb + mt
                        ocol = (base + 2048 + mt * T) if half == 0 else (base + mt * T)
                        for n in range(2):
                            b = next_bank()
                            if sb == 0 and half == 0 and mb == mbs[0]:
                                mm = 0
                                for q in range(4):
                                    pairs = [(wtile(slot, k, 512,
                                                    half * 256 + mt * P),
                                              XBs(k, n * 512, 512))
                                             for k in range(4 * q, 4 * q + 4)]
                                    mm = S.add(
                                        "pe", f_mmgroup_flags(
                                            pb[b][:, :], pairs,
                                            start=(q == 0), stop=(q == 3)),
                                        waits=[ld, ("sXB", 16 * (q + 1)),
                                               bank_free[b]], inc="sPE")
                            else:
                                pairs = [(wtile(slot, k, 512, half * 256 + mt * P),
                                          XBs(k, n * 512, 512)) for k in range(NT)]
                                mm = S.add("pe", f_mmgroup(pb[b][:, :], pairs),
                                           waits=[ld, xb_ld, bank_free[b]],
                                           inc="sPE")
                            ev = S.add("act", f_act(arf(ocol + n * 512, 512),
                                                    pb[b][:, :], func,
                                                    bias=svc(boff, g)),
                                       waits=[("sPE", mm), sv_ld,
                                              ws_ready[mb % 2]], inc="sACT")
                            bank_free[b] = ("sACT", ev)
                            ev_mb[mb].append(ev)
            for bi, mb in enumerate(mbs):
                mark_slot(blks[bi][0], "sACT", max(ev_mb[mb]))

            for mb in mbs:
                base = R0 if mb % 2 == 0 else R1
                dcol = lambda mt, base=base: base + mt * T
                icol = lambda mt, base=base: base + 2048 + mt * T
                ev_all = max(ev_mb[mb])
                for mt in range(2):
                    g = 2 * mb + mt
                    for n in range(2):
                        scr = R2 + 1024
                        S.add("dve", f_ts(arf(scr, 512),
                                          arf(dcol(mt) + n * 512, 512),
                                          -1.0, 1.0, ALU.mult, ALU.add),
                              waits=[("sACT", ev_all)], inc="sDVE")
                        S.add("dve", f_tt(arf(icol(mt) + n * 512, 512),
                                          arf(icol(mt) + n * 512, 512),
                                          arf(scr, 512), ALU.mult), inc="sDVE")
                    S.add("dve", f_scan(Y0s(g), arf(dcol(mt), T),
                                        arf(icol(mt), T), 0.0, ALU.mult, ALU.add),
                          inc="sDVE")
                    S.add("dve", f_scan(AGs(g), arf(dcol(mt), T),
                                        arf(dcol(mt), T), 1.0, ALU.mult, ALU.bypass),
                          inc="sDVE")
                    S.add("dve", f_copy(svc(SV_SFC, g),
                                        arb(Y0_OFF + g * T + T - 1, 1)), inc="sDVE")
                ws_ready[mb % 2] = ("sDVE", S.n("sDVE"))

        scans_all = S.n("sDVE")

        mk = S.add("dve", f_tt(SV[:, SV_SFC:SV_SFC + 16], SV[:, SV_SFC:SV_SFC + 16],
                               SV[:, SV_ME:SV_ME + 16], ALU.mult),
                   waits=[("sDVE", scans_all), sv_ld], inc="sDVE")
        dma("sync", ccin_d.ap().rearrange("(p g) -> p g", p=P),
            SV[:, SV_SFC:SV_SFC + 16], "sCI", waits=[("sDVE", mk)])
        ccin_st = S.n("sCI")

        gate_blocks = [stream_block(w_gat_d[0])]

        gate_free = [("sDVE", 0), ("sDVE", 0)]
        AGc = lambda k, n: arb(AG_OFF + k * T + n * 512, 512)
        Y0n = lambda k, n: arb(Y0_OFF + k * T + n * 512, 512)
        s0_ld = [0]
        for mb in range(8):
            if mb % 2 == 0 and 1 <= mb // 2 <= 3:
                gate_blocks.append(stream_block(w_gat_d[mb // 2]))
            if mb == 6:
                S.add("pool", lambda e: e.collective_compute(
                    "AllReduce", ALU.add,
                    replica_groups=[[0, 1], [2, 3], [4, 5], [6, 7]],
                    ins=[ccin_d.ap()], outs=[ccout_d.ap()]),
                    waits=[("sCI", ccin_st)], inc="sCC")
                dma("sync", SV[:, SV_S0:SV_S0 + 16],
                    ccout_d.ap().rearrange("(p g) -> p g", p=P), "sS0",
                    waits=[("sCC", 1)])
                s0_ld[0] = S.n("sS0")
            s_ga, ld_ga = gate_blocks[mb // 2]
            for mt in range(2):
                g = 2 * mb + mt
                for n in range(2):
                    gcol = R2 + n * 512
                    b = next_bank()
                    pairs = [(wtile(s_ga, k, 512, (mb % 2) * 256 + mt * P),
                              XBs(k, n * 512, 512)) for k in range(NT)]
                    mm = S.add("pe", f_mmgroup(pb[b][:, :], pairs),
                               waits=[ld_ga, xb_ld, bank_free[b]], inc="sPE")
                    ev = S.add("act", f_act(arf(gcol, 512), pb[b][:, :],
                                            AF.Sigmoid, bias=svc(SV_B_GAT, g)),
                               waits=[("sPE", mm), gate_free[n]], inc="sACT")
                    bank_free[b] = ("sACT", ev)
                    S.add("dve", f_tt(Y0n(g, n), Y0n(g, n), arf(gcol, 512),
                                      ALU.mult),
                          waits=[("sACT", ev)], inc="sDVE")
                    ag = S.add("dve", f_tt(AGc(g, n), AGc(g, n), arf(gcol, 512),
                                           ALU.mult), inc="sDVE")
                    gate_free[n] = ("sDVE", ag)
            if mb % 2 == 1:
                mark_slot(s_ga, "sDVE", S.n("sDVE"))

        S.add("dve", f_tt(SV[:, SV_S0:SV_S0 + 16], SV[:, SV_S0:SV_S0 + 16],
                          SV[:, SV_MO:SV_MO + 16], ALU.mult),
              waits=[("sS0", s0_ld[0])], inc="sDVE")
        corr_scr = ARF[:, R2 + 1024:R2 + 1024 + 512].bitcast(BF16)
        for g in range(NT):
            S.add("dve", f_ts(corr_scr, AGs(g), svc(SV_S0, g), None, ALU.mult),
                  inc="sDVE")
            S.add("dve", f_tt(Y0s(g), Y0s(g), corr_scr, ALU.add), inc="sDVE")
        y_done = S.n("sDVE")

        def ln_rows_and_bcast(last_vmm, last_sqmm, extra_dve=0, scratch=None):
            scr = pS0[0:1, :] if scratch is None else scratch
            r1 = S.add("dve", f_ts(STATR[0:1, 0:512], pS0[0:1, :], 1.0 / D, None,
                                   ALU.mult), waits=[("sPE", last_vmm)], inc="sDVE")
            S.add("dve", f_ts(STATR[0:1, 512:1024], pS1[0:1, :], 1.0 / D, None,
                              ALU.mult), waits=[("sPE", last_sqmm)], inc="sDVE")
            S.add("dve", f_tt(scr, STATR[0:1, 0:512],
                              STATR[0:1, 0:512], ALU.mult), inc="sDVE")
            r4 = S.add("dve", f_tt(STATR[0:1, 512:1024], STATR[0:1, 512:1024],
                                   scr, ALU.subtract), inc="sDVE")
            S.add("act", f_act(scr, STATR[0:1, 512:1024],
                               AF.Ln, bias=SV[0:1, SV_CONST:SV_CONST + 1]),
                  waits=[("sDVE", r4)], inc="sACT")
            r6 = S.add("act", f_act(STATR[0:1, 512:1024], scr, AF.Exp,
                                    scale=SV[0:1, SV_CONST + 1:SV_CONST + 2]),
                       inc="sACT")
            r8 = S.add("dve", f_stt(STATR[0:1, 0:512], STATR[0:1, 0:512], -1.0,
                                    STATR[0:1, 512:1024], ALU.mult, ALU.mult),
                       waits=[("sACT", r6)], inc="sDVE")
            bw = [("sACT", r6), ("sDVE", max(ones_done, extra_dve, r8))]
            bR = S.add("pe", f_mm1(pR[:, :], ONES1[0:1, :], STATR[0:1, 512:1024]),
                       waits=bw, inc="sPE")
            bM = S.add("pe", f_mm1(pM[:, :], ONES1[0:1, :], STATR[0:1, 0:512]),
                       waits=[("sDVE", r8)], inc="sPE")
            return max(bR, bM)

        xr_free = [("sDVE", y_done), ("sDVE", y_done)]
        Vbase = [R0, R1]
        stat_after = [0, 0]
        hb_marks = [[], []]
        x1_done = [0, 0]

        def emit_cpre_step(c, m, sub):
            if m % 4 == 0:
                sub["blk"] = stream_block(w_out_d[m // 4])
            s_o, ld_o = sub["blk"]
            xi = m % 2
            xr = R2 + xi * 512
            xl = dma("sync", arf(xr, 512),
                     xT_d[m * P:(m + 1) * P, c * CH:(c + 1) * CH],
                     f"sXR{xi}", waits=[xr_free[xi]])
            b = next_bank()
            pairs = [(wtile(s_o, k, 512, (m % 4) * P), Y0c(k, c)) for k in range(NT)]
            mm = S.add("pe", f_mmgroup(pb[b][:, :], pairs),
                       waits=[ld_o, bank_free[b], ("sDVE", y_done)], inc="sPE")
            vm = S.add("dve", f_stt(arf(Vbase[c] + m * 512, 512), pb[b][:, :],
                                    svc(SV_B_OUT, m), arf(xr, 512),
                                    ALU.add, ALU.add),
                       waits=[("sPE", mm), (f"sXR{xi}", xl)], inc="sDVE")
            bank_free[b] = ("sDVE", vm)
            xr_free[xi] = ("sDVE", vm)
            sub["v_marks"].append(vm)
            stw = [("sDVE", vm)]
            if stat_after[1 - c]:
                stw.append(("sPE", stat_after[1 - c]))
            S.add("act", f_act(STB(m), arf(Vbase[c] + m * 512, 512), AF.Copy),
                  waits=stw, inc="sACT")
            sq = S.add("act", f_act(STBq(m), arf(Vbase[c] + m * 512, 512), AF.Square),
                       inc="sACT")
            sub["stat_marks"].append(sq)
            if m % 4 == 3:
                mark_slot(s_o, "sPE", mm)

        def emit_cstats(c, sub, extra_dve=0):
            ssum = S.add("pe", f_mmgroup(pS0[0:1, :],
                                         [(ONESB[:, 0:1], STB(k)) for k in range(NT)]),
                         waits=[("sACT", max(sub["stat_marks"])),
                                ("sDVE", ones_done)], inc="sPE")
            ssq = S.add("pe", f_mmgroup(pS1[0:1, :],
                                        [(ONESB[:, 0:1], STBq(k)) for k in range(NT)]),
                        inc="sPE")
            stat_after[c] = ssq
            return ln_rows_and_bcast(ssum, ssq, extra_dve=extra_dve)

        def bfly_pair_cb(la, lb, sp):
            S.add("dve", f_tt(arb(sp, CH), arb(la, CH), arb(lb, CH), ALU.subtract),
                  inc="sDVE")
            S.add("dve", f_tt(arb(la, CH), arb(la, CH), arb(lb, CH), ALU.add),
                  inc="sDVE")

        def h128_pair(la, lb, marks):
            dv = S.n("sDVE")
            be = next_bank()
            S.add("pe", f_mm1(pb[be][:, :], H128[:, :], arb(la, CH),
                              start=True, stop=False),
                  waits=[("sDVE", dv), bank_free[be], sv_ld], inc="sPE")
            S.add("pe", f_mm1(pb[be][:, :], H128[:, :], arb(lb, CH),
                              start=False, stop=True), inc="sPE")
            bo = next_bank()
            S.add("pe", f_mm1(pb[bo][:, :], H128[:, :], arb(la, CH),
                              start=True, stop=False),
                  waits=[bank_free[bo]], inc="sPE")
            mmo = S.add("pe", f_mm1(pb[bo][:, :], H128N[:, :], arb(lb, CH),
                                    start=False, stop=True), inc="sPE")
            eve = S.add("act", f_act(arb(la, CH), pb[be][:, :], AF.Copy),
                        waits=[("sPE", mmo)], inc="sACT")
            evo = S.add("act", f_act(arb(lb, CH), pb[bo][:, :], AF.Copy),
                        inc="sACT")
            bank_free[be] = ("sACT", eve)
            bank_free[bo] = ("sACT", evo)
            marks.append(evo)

        hb_loc = [None, None]

        def make_bfly1(c, spare):
            locs = {g: XB_OFF + g * T + c * CH for g in range(NT)}

            def final_cb(p, la, lb):
                h128_pair(la, lb, hb_marks[c])
            bf = Bfly(NT, locs, spare, bfly_pair_cb, final_cb)
            hb_loc[c] = bf.loc
            return bf

        def emit_dpost_step(c, m, sub, bcast):
            tcol = R2 + 1024
            vt = arf(Vbase[c] + m * 512, 512)
            S.add("dve", f_tt(arf(tcol, 512), vt, pR[:, :], ALU.mult),
                  waits=[("sPE", bcast),
                         ("sACT", max(sub["stat_marks"]))], inc="sDVE")
            t2 = S.add("dve", f_tt(vt, arf(tcol, 512), pM[:, :], ALU.add),
                       inc="sDVE")
            S.add("dve", f_ts(HBs(m, c), vt, svc(SV_SUP, m), None, ALU.mult),
                  inc="sDVE")
            sub["bfly"].tile_arrived(m)
            sub["nsteps"] = sub.get("nsteps", 0) + 1
            if sub["nsteps"] == NT:
                x1_done[c] = S.n("sDVE")

        DPERM = [v for p in zip(range(NT // 2), range(NT // 2, NT)) for v in p]

        sub0 = {"v_marks": [], "stat_marks": [], "bfly": make_bfly1(0, SPARE_OFF)}
        for m in range(NT):
            emit_cpre_step(0, m, sub0)
        bcast0 = emit_cstats(0, sub0)

        sub1 = {"v_marks": [], "stat_marks": [],
                "bfly": make_bfly1(1, SPARE_OFF + CH)}
        for m in range(NT):
            emit_cpre_step(1, m, sub1)
            emit_dpost_step(0, DPERM[m], sub0, bcast0)
        bcast1 = emit_cstats(1, sub1, extra_dve=x1_done[0])

        d1_state = {"m": 0}

        def pump_d1():
            m = d1_state["m"]
            if m < NT:
                emit_dpost_step(1, DPERM[m], sub1, bcast1)
                d1_state["m"] = m + 1
            elif m == NT:
                d1_state["store"] = dma(
                    "sync", x1h_d.ap().rearrange("(m p) t -> p m t", p=P),
                    ARF[:, R1:R1 + 8192].rearrange("p (m t) -> p m t", m=NT),
                    "sX1S", waits=[("sDVE", x1_done[1])])
                d1_state["m"] = m + 1

        stg_free = [("sU0", 0), ("sU1", 0)]
        stg_idx = [0]

        def stg_ap(j):
            return ARF[:, R2 + 1024 + j * 256: R2 + 1024 + (j + 1) * 256]\
                .bitcast(BF16)

        vt_marks0 = []
        locs2 = {i: U2_OFF + i * CH for i in range(NI)}
        bfly2 = Bfly(NI, locs2, sub0["bfly"].stale_slot(), bfly_pair_cb,
                     lambda p, la, lb: h128_pair(la, lb, vt_marks0))
        gt_ring = [("sDVE", S.n("sDVE")), ("sDVE", S.n("sDVE"))]
        gi_ctr = [0]
        last_ug_mm = [0]

        def emit_ug_tile(c, tile, s_ug, ld_ug):
            hb_ok = max(hb_marks[c])
            bu = next_bank()
            pairs_u = [(wtile(s_ug, k, 512, (tile % 2) * P),
                        arb(hb_loc[c][k] + 0, CH)) for k in range(NT)]
            mmup = S.add("pe", f_mmgroup(pb[bu][:, :], pairs_u),
                         waits=[ld_ug, bank_free[bu], ("sACT", hb_ok)], inc="sPE")
            bg = next_bank()
            pairs_g = [(wtile(s_ug, k, 512, 256 + (tile % 2) * P),
                        arb(hb_loc[c][k] + 0, CH)) for k in range(NT)]
            mmgt = S.add("pe", f_mmgroup(pb[bg][:, :], pairs_g),
                         waits=[ld_ug, bank_free[bg], ("sACT", hb_ok)], inc="sPE")
            last_ug_mm[0] = mmgt
            gi = gi_ctr[0] % 2
            gi_ctr[0] += 1
            gcol = R2 + gi * 512
            sg = S.add("act", f_act(arf(gcol, 512), pb[bg][:, :], AF.Silu,
                                    scale=svc(SV_WS_GT, tile)),
                       waits=[("sPE", mmgt), gt_ring[gi], sv_ld], inc="sACT")
            if c == 0:
                u2 = S.add("dve", f_stt(U2s(tile), pb[bu][:, :], svc(SV_WSU2, tile),
                                        arf(gcol, 512), ALU.mult, ALU.mult),
                           waits=[("sPE", mmup), ("sACT", sg), ("sDVE", sv_pre)],
                           inc="sDVE")
            else:
                j = stg_idx[0] % 2
                stg_idx[0] += 1
                u2 = S.add("dve", f_stt(stg_ap(j), pb[bu][:, :], svc(SV_WSU2, tile),
                                        arf(gcol, 512), ALU.mult, ALU.mult),
                           waits=[("sPE", mmup), ("sACT", sg), ("sDVE", sv_pre),
                                  stg_free[j]], inc="sDVE")
                d = tile // 2
                sp = DPOS[d] if tile % 2 == 0 else 32 + DPOS[d]
                n = dma("sync", u2h_d[sp], stg_ap(j), f"sU{j}",
                        waits=[("sDVE", u2)])
                stg_free[j] = (f"sU{j}", n)
            bank_free[bu] = ("sDVE", u2)
            bank_free[bg] = ("sDVE", u2)
            gt_ring[gi] = ("sDVE", u2)
            if c == 0:
                bfly2.tile_arrived(tile)
            return u2

        border = [b for pr in zip(range(NI // 4), range(NI // 4, NI // 2))
                  for b in pr]
        DORDER = [d for pr in zip(range(16), range(16, 32)) for d in pr]
        DPOS = {d: i for i, d in enumerate(DORDER)}
        for pos in range(32):
            b = border[pos]
            if pos == 8:
                while d1_state["m"] <= NT:
                    pump_d1()
            s_ug, ld_ug = stream_block(w_ug_d[b])
            for tile in (2 * b, 2 * b + 1):
                emit_ug_tile(0, tile, s_ug, ld_ug)
                if 8 <= pos < 30:
                    emit_ug_tile(1, tile, s_ug, ld_ug)
            mark_slot(s_ug, "sDVE", S.n("sDVE"))
            if pos < 8:
                pump_d1()
                pump_d1()
        for b in border[:8] + border[30:]:
            s_ug, ld_ug = stream_block(w_ug_d[b])
            for tile in (2 * b, 2 * b + 1):
                emit_ug_tile(1, tile, s_ug, ld_ug)
            mark_slot(s_ug, "sDVE", S.n("sDVE"))
        vt_ok0 = max(vt_marks0)
        init_spare0 = sub0["bfly"].stale_slot()
        reloc_mark = 0
        for t, lo in list(bfly2.loc.items()):
            if lo == init_spare0:
                stale = bfly2.stale_slot()
                reloc_mark = S.add(
                    "dve", f_copy(arb(stale, CH), arb(lo, CH)),
                    waits=[("sACT", vt_ok0)], inc="sDVE")
                bfly2.loc[t] = stale
        u2_last_mark = S.n("sDVE")

        def emit_down_ln2(c, loc_of, vt_ok, x1_ready, per_m_pre=None,
                          f_extra=(), mm_extra=()):
            f_free = ("sDVE", 0)
            sq_free = [("sPE", 0), ("sPE", 0)]
            v2_marks = []
            last_vmm = last_sqmm = 0
            down_last = [0]
            for m in range(NT):
                s_dn, ld_dn = stream_block(w_dn_d[m])
                b = next_bank()
                pairs = [(wtile(s_dn, k, 128, 0), arb(loc_of[k], CH))
                         for k in range(NI)]
                mm = S.add("pe", f_mmgroup(pb[b][:, :], pairs),
                           waits=[ld_dn, bank_free[b], ("sACT", vt_ok)]
                                 + list(mm_extra), inc="sPE")
                down_last[0] = mm
                extra = list(per_m_pre(m)) if per_m_pre is not None else []
                fi = m % 2
                fcol = R2 + 1024
                fw = [("sPE", mm), f_free]
                if m < 1:
                    fw.append(("sDVE", S.n("sDVE")))
                    fw += list(f_extra)
                fe = S.add("act", f_act(arf(fcol, 512), pb[b][:, :], AF.Copy,
                                        scale=svc(SV_WS_DP, m)), waits=fw, inc="sACT")
                bank_free[b] = ("sACT", fe)
                v2w = [("sACT", fe), x1_ready] + extra
                src = R0 if c == 0 else R1
                v2 = S.add("dve", f_tt(arf(R1 + m * 512, 512),
                                       arf(src + m * 512, 512),
                                       arf(fcol, 512), ALU.add),
                           waits=v2w, inc="sDVE")
                f_free = ("sDVE", v2)
                v2_marks.append(v2)
                last_vmm = S.add("pe", f_mm1(pS0[0:1, :], ONESF[:, 0:1],
                                             arf(R1 + m * 512, 512),
                                             start=(m == 0), stop=(m == NT - 1)),
                                 waits=[("sDVE", v2), ("sDVE", ones_done)], inc="sPE")
                sqw = [("sDVE", v2), sq_free[fi]]
                if m < 2:
                    sqw.append(("sDVE", u2_last_mark))
                sq = S.add("act", f_act(arf(R2 + fi * 512, 512),
                                        arf(R1 + m * 512, 512), AF.Square),
                           waits=sqw, inc="sACT")
                last_sqmm = S.add("pe", f_mm1(pS1[0:1, :], ONESF[:, 0:1],
                                              arf(R2 + fi * 512, 512),
                                              start=(m == 0), stop=(m == NT - 1)),
                                  waits=[("sACT", sq)], inc="sPE")
                sq_free[fi] = ("sPE", last_sqmm)
                mark_slot(s_dn, "sPE", mm)
            return down_last[0], v2_marks, last_vmm, last_sqmm

        def emit_norm_step(c, m, bcast, v2_all, store_eng, obase=R1):
            t1 = S.add("dve", f_tt(arf(obase + m * 512, 512),
                                   arf(R1 + m * 512, 512),
                                   pR[:, :], ALU.mult),
                       waits=[("sPE", bcast), ("sDVE", v2_all)], inc="sDVE")
            t2 = S.add("dve", f_tt(arf(obase + m * 512, 512),
                                   arf(obase + m * 512, 512),
                                   pM[:, :], ALU.add), inc="sDVE")
            stn = dma(store_eng, out_d[m * P:(m + 1) * P, c * CH:(c + 1) * CH],
                      arf(obase + m * 512, 512), f"sST{m % 2}",
                      waits=[("sDVE", t2)])
            return t1, t2, stn

        ev_locs = {d: XB_OFF + DPOS[d] * 512 for d in range(32)}
        od_locs = {d: U2_OFF + DPOS[d] * 512 for d in range(32)}
        ev_final = {}
        vt_marks1 = []
        loc1_final = {}

        def ev_done(d, loc, _):
            ev_final[d] = loc

        def od_done(d, loc, _):
            la = ev_final[d]
            h128_pair(la, loc, vt_marks1)
            loc1_final[2 * d] = la
            loc1_final[2 * d + 1] = loc
        bfe = Bfly(32, ev_locs, SPARE_OFF, bfly_pair_cb, ev_done, absorb_h1=False)
        bfo = Bfly(32, od_locs, SPARE_OFF + 512, bfly_pair_cb, od_done,
                   absorb_h1=False)

        ev_dmas = []
        for g in range(4):
            n = dma("sync",
                    ARB[:, XB_OFF + g * 8 * 512: XB_OFF + (g + 1) * 8 * 512]
                    .rearrange("p (i t) -> p i t", i=8),
                    u2h_d[g * 8:(g + 1) * 8].rearrange("i p t -> p i t"),
                    f"sU{g % 2}",
                    waits=[("sPE", last_ug_mm[0]),
                           ("sU0", S.n("sU0")), ("sU1", S.n("sU1"))])
            ev_dmas.append((f"sU{g % 2}", n))
        ev_state = {"g": 0}

        def pump_even(m):
            if m % 4 == 0 and ev_state["g"] < 4:
                g = ev_state["g"]
                ev_state["g"] = g + 1
                S.add("dve", lambda e: e.nop(), waits=[ev_dmas[g]], inc="sDVE")
                for i in range(g * 8, (g + 1) * 8):
                    bfe.tile_arrived(DORDER[i])
            return []

        x1_ready0 = ("sX1S", d1_state["store"])
        dn0_last, v2m0, lvmm0, lsqm0 = emit_down_ln2(
            0, bfly2.loc, vt_ok0, x1_ready0, per_m_pre=pump_even,
            f_extra=[stg_free[0], stg_free[1]], mm_extra=[("sDVE", reloc_mark)])

        od_dmas = []
        for g in range(4):
            n = dma("pool",
                    ARB[:, U2_OFF + g * 8 * 512: U2_OFF + (g + 1) * 8 * 512]
                    .rearrange("p (i t) -> p i t", i=8),
                    u2h_d[32 + g * 8: 32 + (g + 1) * 8].rearrange("i p t -> p i t"),
                    f"sU{g % 2}",
                    waits=[("sPE", dn0_last),
                           (f"sU{g % 2}", S.n(f"sU{g % 2}"))])
            S.add("dve", lambda e: e.nop(), waits=[(f"sU{g % 2}", n)], inc="sDVE")
            for i in range(g * 8, (g + 1) * 8):
                bfo.tile_arrived(DORDER[i])
        vt_ok1 = max(vt_marks1)

        bcast0 = ln_rows_and_bcast(lvmm0, lsqm0, extra_dve=x1_done[1],
                                   scratch=pR[0:1, :])
        x1slab = {}

        def emit_x1_slab(m):
            t1, t2, _ = emit_norm_step(0, m, bcast0, max(v2m0), "sync", obase=R0)
            n = dma("sync", arf(R1 + m * 512, 512),
                    x1h_d[m * P:(m + 1) * P, :], "sX1",
                    waits=[("sDVE", t1), ("sX1", S.n("sX1")),
                           ("sX1S", d1_state["store"])])
            x1slab[m] = ("sX1", n)

        emit_x1_slab(0)
        emit_x1_slab(1)

        def pre_p1(m):
            if m + 2 < NT:
                emit_x1_slab(m + 2)
            return [x1slab[m]]

        dn1_last, v2m1, lvmm1, lsqm1 = emit_down_ln2(
            1, loc1_final, vt_ok1, ("sX1", 0), per_m_pre=pre_p1)

        bcast1 = ln_rows_and_bcast(lvmm1, lsqm1)
        for m in range(NT):
            emit_norm_step(1, m, bcast1, max(v2m1), "sync")

        S.add("sync", lambda e: e.nop(),
              waits=[("sST0", S.n("sST0")), ("sST1", S.n("sST1"))])

        @block.sync
        def _(sy):
            S.replay("sync", sy, sems)

        @block.gpsimd
        def _(g):
            S.replay("pool", g, sems)

        @block.tensor
        def _(t):
            S.replay("pe", t, sems)

        @block.scalar
        def _(sc):
            S.replay("act", sc, sems)

        @block.vector
        def _(v):
            S.replay("dve", v, sems)

    return nc


def _vec16(v):
    return np.ascontiguousarray(v.reshape(16, P).T)


def _vec64(v):
    return np.ascontiguousarray(v.reshape(64, P).T)


def _blocks(W, nblk, width):
    K, M = W.shape
    assert M == nblk * width
    return np.ascontiguousarray(W.reshape(K, nblk, width).transpose(1, 0, 2))


_NC_CACHE = {}


def kernel(L0_attn_x, L0_attn_W_in, L0_attn_b_in, L0_attn_W_decay, L0_attn_b_decay,
           L0_attn_W_gate, L0_attn_b_gate, L0_attn_W_out, L0_attn_b_out,
           L0_attn_decay_bias, L0_ffn_SU_ug, L0_ffn_SU_dp, L0_ffn_Wscale_ug,
           L0_ffn_Wscale_dp, L0_ffn_W_up, L0_ffn_W_gate, L0_ffn_W_down):
    x = np.asarray(L0_attn_x, np.float32)
    f = lambda a: np.asarray(a, np.float32)

    bi_ = _blocks(f(L0_attn_W_in), 8, 256)
    bd_ = _blocks(f(L0_attn_W_decay), 8, 256)
    w_ind = np.ascontiguousarray(np.concatenate([bi_, bd_], axis=2))
    w_gat = _blocks(f(L0_attn_W_gate), 4, 512)
    w_out = _blocks(f(L0_attn_W_out), 4, 512)
    bu2 = _blocks(np.ascontiguousarray(f(L0_ffn_W_up).T), 32, 256)
    bg2 = _blocks(np.ascontiguousarray(f(L0_ffn_W_gate).T), 32, 256)
    w_ug = np.ascontiguousarray(np.concatenate([bu2, bg2], axis=2))
    w_dn = _blocks(np.ascontiguousarray(f(L0_ffn_W_down).T), 16, 128)

    h128 = hadamard(P).astype(ml_dtypes.bfloat16)
    h128n = (-hadamard(P)).astype(ml_dtypes.bfloat16)

    sv_base = np.zeros((P, SV_W), np.float32)
    sv_base[:, SV_B_IN:SV_B_IN + 16] = _vec16(f(L0_attn_b_in))
    sv_base[:, SV_B_DEC:SV_B_DEC + 16] = _vec16(f(L0_attn_b_decay))
    sv_base[:, SV_DBIAS:SV_DBIAS + 16] = _vec16(f(L0_attn_decay_bias))
    sv_base[:, SV_B_GAT:SV_B_GAT + 16] = _vec16(f(L0_attn_b_gate))
    sv_base[:, SV_B_OUT:SV_B_OUT + 16] = _vec16(f(L0_attn_b_out))
    sv_base[:, SV_SU_UG:SV_SU_UG + 16] = _vec16(f(L0_ffn_SU_ug))
    sv_base[:, SV_WS_DP:SV_WS_DP + 16] = _vec16(f(L0_ffn_Wscale_dp))
    sv_base[:, SV_WS_UP:SV_WS_UP + 64] = _vec64(f(L0_ffn_Wscale_ug)[:I])
    sv_base[:, SV_WS_GT:SV_WS_GT + 64] = _vec64(f(L0_ffn_Wscale_ug)[I:])
    sv_base[:, SV_SU_DP:SV_SU_DP + 64] = _vec64(f(L0_ffn_SU_dp))
    sv_base[:, SV_CONST] = EPS
    sv_base[:, SV_CONST + 1] = -0.5

    if "prog" not in _NC_CACHE:
        _NC_CACHE["prog"] = build_program()
    nc = _NC_CACHE["prog"]

    in_maps = []
    for c in range(8):
        b, h = c // 2, c % 2
        xT = np.ascontiguousarray(x[b, h * T:(h + 1) * T, :].T)
        sv = sv_base.copy()
        sv[:, SV_ME:SV_ME + 16] = 1.0 if h == 0 else 0.0
        sv[:, SV_MO:SV_MO + 16] = 1.0 if h == 1 else 0.0
        in_maps.append({
            "xT": xT, "w_ind": w_ind, "w_gat": w_gat, "w_out": w_out,
            "w_ug": w_ug, "w_dn": w_dn, "sv": sv, "h128": h128,
            "h128n": h128n,
        })

    res = run_bass_kernel_spmd(nc, in_maps, core_ids=list(range(8)))

    out = np.empty((4, 2048, D), np.float32)
    for c in range(8):
        b, h = c // 2, c % 2
        out[b, h * T:(h + 1) * T, :] = res.results[c]["out"].T
    return out


if __name__ == "__main__":
    import reference
    ins = reference.setup_inputs()
    got = kernel(**{k: np.asarray(v) for k, v in ins.items()})
    exp = np.asarray(reference.reference(**ins))
    err = np.abs(got - exp)
    rel = np.sqrt(((got - exp) ** 2).mean() / (exp ** 2).mean())
    print("maxabs", err.max(), "relL2", rel)


# version 21 (final)
# speedup vs baseline: 1.1091x; 1.0030x over previous
import numpy as np
import ml_dtypes
from contextlib import ExitStack

import concourse.bass as bass
import concourse.mybir as mybir
from concourse.bass_utils import run_bass_kernel_spmd

F32 = mybir.dt.float32
BF16 = mybir.dt.bfloat16
AF = mybir.ActivationFunctionType
ALU = mybir.AluOpType

P = 128
D = 2048
I = 8192
T = 1024
NT = D // P
NI = I // P
CH = 512
NCH = T // CH
EPS = 1e-5

SV_B_IN, SV_B_DEC, SV_DBIAS, SV_B_GAT, SV_B_OUT = 0, 16, 32, 48, 64
SV_SU_UG, SV_WS_DP, SV_ME, SV_MO = 80, 96, 112, 128
SV_WS_UP, SV_WS_GT, SV_SU_DP = 144, 208, 272
SV_BIAS2, SV_SUP, SV_WSU2, SV_SFC, SV_S0 = 336, 352, 368, 432, 448
SV_CONST = 464
SV_W = 468

R0, R1, R2 = 0, 8192, 16384
ARF_W = 17920
XB_OFF, Y0_OFF, AG_OFF = 0, 16384, 32768
U2_OFF = 16384
SPARE_OFF = 49152
ARB_W = 50176

ENGS = ("sync", "pool", "pe", "act", "dve")


class Sched:
    def __init__(self):
        self.items = {e: [] for e in ENGS}
        self.cnt = {}

    def n(self, sem):
        return self.cnt.get(sem, 0)

    def add(self, eng, fn, waits=(), inc=None, inc_by=1):
        w = {}
        for s, v in waits:
            if v > 0:
                w[s] = max(w.get(s, 0), v)
        after = None
        if inc is not None:
            self.cnt[inc] = self.cnt.get(inc, 0) + inc_by
            after = self.cnt[inc]
        self.items[eng].append((fn, sorted(w.items()), inc, inc_by))
        return after

    def replay(self, eng_name, eng_obj, sems):
        for fn, waits, inc, inc_by in self.items[eng_name]:
            for s, v in waits:
                eng_obj.wait_ge(sems[s], v)
            r = fn(eng_obj)
            if inc is not None:
                if r is None:
                    r = eng_obj.nop()
                r.then_inc(sems[inc], inc_by)


def f_tt(o, a, b, op):
    return lambda e, o=o, a=a, b=b, op=op: e.tensor_tensor(out=o, in0=a, in1=b, op=op)


def f_ts(o, a, s1, s2, op0, op1=None):
    if op1 is None:
        return lambda e, o=o, a=a, s1=s1: e.tensor_scalar(
            out=o, in0=a, scalar1=s1, scalar2=None, op0=ALU.mult)
    return lambda e, o=o, a=a, s1=s1, s2=s2, op0=op0, op1=op1: e.tensor_scalar(
        out=o, in0=a, scalar1=s1, scalar2=s2, op0=op0, op1=op1)


def f_stt(o, a, sc, b, op0, op1):
    return lambda e, o=o, a=a, sc=sc, b=b, op0=op0, op1=op1: e.scalar_tensor_tensor(
        out=o, in0=a, scalar=sc, in1=b, op0=op0, op1=op1)


def f_act(o, a, func, bias=None, scale=None):
    def run(e, o=o, a=a, func=func, bias=bias, scale=scale):
        kw = {}
        if bias is not None:
            kw["bias"] = bias
        if scale is not None:
            kw["scale"] = scale
        return e.activation(out=o, in_=a, func=func, **kw)
    return run


def f_scan(o, d0, d1, init, op0, op1):
    return lambda e, o=o, d0=d0, d1=d1, init=init, op0=op0, op1=op1: \
        e.tensor_tensor_scan(out=o, data0=d0, data1=d1, initial=init, op0=op0, op1=op1)


def f_copy(o, a):
    return lambda e, o=o, a=a: e.tensor_copy(o, a)


def f_mmgroup(out_ap, pairs):
    def run(e, out_ap=out_ap, pairs=pairs):
        last = len(pairs) - 1
        r = None
        for j, (l, rh) in enumerate(pairs):
            r = e.matmul(out_ap, l, rh, start=(j == 0), stop=(j == last))
        return r
    return run


def f_mmgroup_flags(out_ap, pairs, start, stop):
    def run(e, out_ap=out_ap, pairs=pairs, start=start, stop=stop):
        last = len(pairs) - 1
        r = None
        for j, (l, rh) in enumerate(pairs):
            r = e.matmul(out_ap, l, rh, start=(start and j == 0),
                         stop=(stop and j == last))
        return r
    return run


def f_mm1(out_ap, l, rh, start=True, stop=True):
    return lambda e, o=out_ap, l=l, rh=rh, st=start, sp=stop: e.matmul(
        o, l, rh, start=st, stop=sp)


def f_memset(ap, val):
    return lambda e, ap=ap, val=val: e.memset(ap, val)


def hadamard(n):
    H = np.array([[1.0]], dtype=np.float64)
    while H.shape[0] < n:
        H = np.block([[H, H], [H, -H]])
    return H


class Bfly:

    def __init__(self, ntiles, locs, spare, pair_cb, final_cb, absorb_h1=True):
        self.n = ntiles
        self.loc = dict(locs)
        self.spare = spare
        self.pair_cb = pair_cb
        self.final_cb = final_cb
        self.absorb_h1 = absorb_h1
        self.done = set()
        self.arrived = set()
        self.final_tiles = set()
        self.fired = set()

    def _dep_ok(self, j, h):
        if 2 * h >= self.n:
            return j in self.arrived and (j + h) in self.arrived
        d0 = (j & ~(2 * h), 2 * h)
        d1 = ((j + h) & ~(2 * h), 2 * h)
        return d0 in self.done and d1 in self.done

    def stale_slot(self):
        lv = self.n.bit_length() - (2 if self.absorb_h1 else 1)
        assert len(self.done) == (self.n // 2) * lv
        return self.spare

    def tile_arrived(self, t):
        self.arrived.add(t)
        hmin = 2 if self.absorb_h1 else 1
        prog = True
        while prog:
            prog = False
            h = self.n // 2
            while h >= hmin:
                for j in range(self.n):
                    if (j & h) == 0 and (j, h) not in self.done and self._dep_ok(j, h):
                        la, lb = self.loc[j], self.loc[j + h]
                        self.pair_cb(la, lb, self.spare)
                        self.loc[j + h] = self.spare
                        self.spare = lb
                        self.done.add((j, h))
                        if h == hmin:
                            if self.absorb_h1:
                                self.final_tiles.add(j)
                                self.final_tiles.add(j + 2)
                                for p in (j & ~1, (j + 2) & ~1):
                                    if (p in self.final_tiles
                                            and p + 1 in self.final_tiles
                                            and p not in self.fired):
                                        self.fired.add(p)
                                        self.final_cb(p, self.loc[p],
                                                      self.loc[p + 1])
                            else:
                                self.final_cb(j, self.loc[j], None)
                                self.final_cb(j + h, self.loc[j + h], None)
                        prog = True
                h //= 2


def build_program():
    nc = bass.Bass("TRN2", target_bir_lowering=False, debug=False)

    xT_d = nc.dram_tensor("xT", [D, T], F32, kind="ExternalInput")
    w_ind_d = nc.dram_tensor("w_ind", [8, D, 512], F32, kind="ExternalInput")
    w_gat_d = nc.dram_tensor("w_gat", [4, D, 512], F32, kind="ExternalInput")
    w_out_d = nc.dram_tensor("w_out", [4, D, 512], F32, kind="ExternalInput")
    w_ug_d = nc.dram_tensor("w_ug", [32, D, 512], F32, kind="ExternalInput")
    w_dn_d = nc.dram_tensor("w_dn", [16, I, 128], F32, kind="ExternalInput")
    sv_d = nc.dram_tensor("sv", [P, SV_W], F32, kind="ExternalInput")
    h128_d = nc.dram_tensor("h128", [P, P], BF16, kind="ExternalInput")
    h128n_d = nc.dram_tensor("h128n", [P, P], BF16, kind="ExternalInput")
    out_d = nc.dram_tensor("out", [D, T], F32, kind="ExternalOutput")
    x1h_d = nc.dram_tensor("x1h", [D, CH], F32)
    u2h_d = nc.dram_tensor("u2h", [NI, P, CH], BF16)
    wc_ug_d = nc.dram_tensor("wc_ug", [32, P, 8192], BF16)
    wc_dn_d = nc.dram_tensor("wc_dn", [16, P, 8192], BF16)
    ccin_d = nc.dram_tensor("ccin", [D], F32)
    ccout_d = nc.dram_tensor("ccout", [D], F32)

    S = Sched()

    with ExitStack() as ctx:
        sbuf = lambda n, sh, dt: ctx.enter_context(nc.sbuf_tensor(n, sh, dt))
        psumt = lambda n, sh: ctx.enter_context(nc.psum_tensor(n, sh, F32))
        mksem = lambda n: ctx.enter_context(nc.semaphore(n))

        ARB = sbuf("ARB", [P, ARB_W], BF16)
        ARF = sbuf("ARF", [P, ARF_W], F32)
        WR = sbuf("WR", [P, 16384], BF16)
        SV = sbuf("SV", [P, SV_W], F32)
        H128 = sbuf("H128", [P, P], BF16)
        H128N = sbuf("H128N", [P, P], BF16)
        ONESB = sbuf("ONESB", [P, 1], BF16)
        ONESF = sbuf("ONESF", [P, 1], F32)
        ONES1 = sbuf("ONES1", [1, P], F32)
        STATR = sbuf("STATR", [1, 1024], F32)

        pb = [psumt(f"pb{i}", [P, 512]) for i in range(4)]
        pR = psumt("pR", [P, 512])
        pM = psumt("pM", [P, 512])
        pS0 = psumt("pS0", [1, 512])
        pS1 = psumt("pS1", [1, 512])

        sems = {n: mksem(n) for n in
                ("sSP", "sXB", "sW0", "sW1", "sW2", "sW3", "sXR0", "sXR1",
                 "sX1", "sX1S", "sCI", "sS0", "sST0", "sST1",
                 "sU0", "sU1", "sCC", "sPE", "sACT", "sDVE")}
        block = ctx.enter_context(nc.Block())

        arb = lambda off, w: ARB[:, off:off + w]
        arf = lambda off, w: ARF[:, off:off + w]
        XBs = lambda k, a, w: arb(XB_OFF + k * T + a, w)
        HBs = lambda k, c: arb(XB_OFF + k * T + c * CH, CH)
        Y0s = lambda k: arb(Y0_OFF + k * T, T)
        Y0c = lambda k, c: arb(Y0_OFF + k * T + c * CH, CH)
        AGs = lambda k: arb(AG_OFF + k * T, T)
        U2s = lambda i: arb(U2_OFF + i * CH, CH)
        STB = lambda m: arb(AG_OFF + m * CH, CH)
        STBq = lambda m: arb(AG_OFF + 8192 + m * CH, CH)
        svc = lambda off, g: SV[:, off + g:off + g + 1]

        def dma(eng, out_ap, in_ap, sem, waits=()):
            fn = lambda e, o=out_ap, i=in_ap: e.dma_start(out=o, in_=i)
            return S.add(eng, fn, waits=waits, inc=sem, inc_by=16)

        dma("sync", SV[:, :], sv_d[:, :], "sSP")
        dma("sync", H128[:, :], h128_d[:, :], "sSP")
        dma("sync", H128N[:, :], h128n_d[:, :], "sSP")
        sv_ld = ("sSP", 48)

        S.add("dve", f_memset(ONESB[:, :], 1.0), inc="sDVE")
        S.add("dve", f_memset(ONESF[:, :], 1.0), inc="sDVE")
        S.add("dve", f_memset(ONES1[:, :], 1.0), inc="sDVE")
        ones_done = S.n("sDVE")

        def emit_xb_quarters():
            for q in range(4):
                dma("pool",
                    ARB[:, XB_OFF + q * 4 * T: XB_OFF + (q + 1) * 4 * T]
                    .rearrange("p (k t) -> p k t", k=4),
                    xT_d[q * 4 * P:(q + 1) * 4 * P, :]
                    .rearrange("(k p) t -> p k t", p=P), "sXB")
        xb_ld = ("sXB", 64)

        S.add("dve", f_tt(SV[:, SV_BIAS2:SV_BIAS2 + 16], SV[:, SV_B_DEC:SV_B_DEC + 16],
                          SV[:, SV_DBIAS:SV_DBIAS + 16], ALU.add),
              waits=[sv_ld], inc="sDVE")
        S.add("dve", f_ts(SV[:, SV_SUP:SV_SUP + 16], SV[:, SV_SU_UG:SV_SU_UG + 16],
                          1.0 / np.sqrt(D), None, ALU.mult), inc="sDVE")
        S.add("dve", f_tt(SV[:, SV_WSU2:SV_WSU2 + 64], SV[:, SV_WS_UP:SV_WS_UP + 64],
                          SV[:, SV_SU_DP:SV_SU_DP + 64], ALU.mult), inc="sDVE")
        S.add("dve", f_ts(SV[:, SV_WSU2:SV_WSU2 + 64], SV[:, SV_WSU2:SV_WSU2 + 64],
                          1.0 / np.sqrt(I), None, ALU.mult), inc="sDVE")
        sv_pre = S.n("sDVE")

        slot_free = [[("sW0", 0)], [("sW1", 0)], [("sW2", 0)], [("sW3", 0)]]
        next_slot = [0]

        def stream_block(dram_ap, two=True):
            s = next_slot[0]
            if two and s % 2 == 1:
                s = (s + 1) % 4
            k = dram_ap.shape[0] // P
            width = k * dram_ap.shape[1]
            assert width in (4096, 8192)
            out_ap = WR[:, s * 4096:s * 4096 + width].rearrange("p (k c) -> p k c", k=k)
            in_ap = dram_ap.rearrange("(k p) c -> p k c", p=P)
            w = list(slot_free[s])
            if two:
                w += slot_free[(s + 1) % 4]
            sem = f"sW{s}"
            n = dma("pool", out_ap, in_ap, sem, waits=w)
            next_slot[0] = (s + (2 if two else 1)) % 4
            return s, (sem, n)

        def mark_slot(slot, sem, cnt, two=True):
            slot_free[slot] = [(sem, cnt)]
            if two:
                slot_free[(slot + 1) % 4] = [(sem, cnt)]

        def wtile(slot, k, cwidth, c0):
            base = slot * 4096 + k * cwidth + c0
            return WR[:, base:base + P]

        bank_free = [("sPE", 0)] * 4
        bank_i = [0]

        def next_bank():
            b = bank_i[0]
            bank_i[0] = (b + 1) % 4
            return b

        ws_ready = [("sDVE", 0), ("sDVE", 0)]
        for sb in range(4):
            mbs = (2 * sb, 2 * sb + 1)
            if sb == 0:
                blks = [stream_block(w_ind_d[mbs[0]])]
                emit_xb_quarters()
                blks.append(stream_block(w_ind_d[mbs[1]]))
            else:
                blks = [stream_block(w_ind_d[mb]) for mb in mbs]
            ev_mb = {mb: [] for mb in mbs}
            for func, boff, half in ((AF.Silu, SV_B_IN, 0), (AF.Sigmoid, SV_BIAS2, 1)):
                for bi, mb in enumerate(mbs):
                    slot, ld = blks[bi]
                    base = R0 if mb % 2 == 0 else R1
                    for mt in range(2):
                        g = 2 * mb + mt
                        ocol = (base + 2048 + mt * T) if half == 0 else (base + mt * T)
                        for n in range(2):
                            b = next_bank()
                            if sb == 0 and half == 0 and mb == mbs[0]:
                                mm = 0
                                for q in range(4):
                                    pairs = [(wtile(slot, k, 512,
                                                    half * 256 + mt * P),
                                              XBs(k, n * 512, 512))
                                             for k in range(4 * q, 4 * q + 4)]
                                    mm = S.add(
                                        "pe", f_mmgroup_flags(
                                            pb[b][:, :], pairs,
                                            start=(q == 0), stop=(q == 3)),
                                        waits=[ld, ("sXB", 16 * (q + 1)),
                                               bank_free[b]], inc="sPE")
                            else:
                                pairs = [(wtile(slot, k, 512, half * 256 + mt * P),
                                          XBs(k, n * 512, 512)) for k in range(NT)]
                                mm = S.add("pe", f_mmgroup(pb[b][:, :], pairs),
                                           waits=[ld, xb_ld, bank_free[b]],
                                           inc="sPE")
                            ev = S.add("act", f_act(arf(ocol + n * 512, 512),
                                                    pb[b][:, :], func,
                                                    bias=svc(boff, g)),
                                       waits=[("sPE", mm), sv_ld,
                                              ws_ready[mb % 2]], inc="sACT")
                            bank_free[b] = ("sACT", ev)
                            ev_mb[mb].append(ev)
            for bi, mb in enumerate(mbs):
                mark_slot(blks[bi][0], "sACT", max(ev_mb[mb]))

            for mb in mbs:
                base = R0 if mb % 2 == 0 else R1
                dcol = lambda mt, base=base: base + mt * T
                icol = lambda mt, base=base: base + 2048 + mt * T
                ev_all = max(ev_mb[mb])
                for mt in range(2):
                    g = 2 * mb + mt
                    for n in range(2):
                        scr = R2 + 1024
                        S.add("dve", f_ts(arf(scr, 512),
                                          arf(dcol(mt) + n * 512, 512),
                                          -1.0, 1.0, ALU.mult, ALU.add),
                              waits=[("sACT", ev_all)], inc="sDVE")
                        S.add("dve", f_tt(arf(icol(mt) + n * 512, 512),
                                          arf(icol(mt) + n * 512, 512),
                                          arf(scr, 512), ALU.mult), inc="sDVE")
                    S.add("dve", f_scan(Y0s(g), arf(dcol(mt), T),
                                        arf(icol(mt), T), 0.0, ALU.mult, ALU.add),
                          inc="sDVE")
                    S.add("dve", f_scan(AGs(g), arf(dcol(mt), T),
                                        arf(dcol(mt), T), 1.0, ALU.mult, ALU.bypass),
                          inc="sDVE")
                    S.add("dve", f_copy(svc(SV_SFC, g),
                                        arb(Y0_OFF + g * T + T - 1, 1)), inc="sDVE")
                ws_ready[mb % 2] = ("sDVE", S.n("sDVE"))

        scans_all = S.n("sDVE")

        mk = S.add("dve", f_tt(SV[:, SV_SFC:SV_SFC + 16], SV[:, SV_SFC:SV_SFC + 16],
                               SV[:, SV_ME:SV_ME + 16], ALU.mult),
                   waits=[("sDVE", scans_all), sv_ld], inc="sDVE")
        dma("sync", ccin_d.ap().rearrange("(p g) -> p g", p=P),
            SV[:, SV_SFC:SV_SFC + 16], "sCI", waits=[("sDVE", mk)])
        ccin_st = S.n("sCI")

        gate_blocks = [stream_block(w_gat_d[0])]

        gate_free = [("sDVE", 0), ("sDVE", 0)]
        corr_state = {"todo": list(range(NT))}
        corr_scr = ARF[:, R2 + 1024:R2 + 1024 + 512].bitcast(BF16)

        def emit_corr(g):
            S.add("dve", f_ts(corr_scr, AGs(g), svc(SV_S0, g), None, ALU.mult),
                  inc="sDVE")
            S.add("dve", f_tt(Y0s(g), Y0s(g), corr_scr, ALU.add), inc="sDVE")

        AGc = lambda k, n: arb(AG_OFF + k * T + n * 512, 512)
        Y0n = lambda k, n: arb(Y0_OFF + k * T + n * 512, 512)
        gateb = lambda n: ARF[:, R2 + n * 256:R2 + (n + 1) * 256].bitcast(BF16)
        s0_ld = [0]
        for mb in range(8):
            if mb % 2 == 0 and 1 <= mb // 2 <= 3:
                gate_blocks.append(stream_block(w_gat_d[mb // 2]))
            if mb == 6:
                S.add("pool", lambda e: e.collective_compute(
                    "AllReduce", ALU.add,
                    replica_groups=[[0, 1], [2, 3], [4, 5], [6, 7]],
                    ins=[ccin_d.ap()], outs=[ccout_d.ap()]),
                    waits=[("sCI", ccin_st)], inc="sCC")
                dma("sync", SV[:, SV_S0:SV_S0 + 16],
                    ccout_d.ap().rearrange("(p g) -> p g", p=P), "sS0",
                    waits=[("sCC", 1)])
                s0_ld[0] = S.n("sS0")
            s_ga, ld_ga = gate_blocks[mb // 2]
            for mt in range(2):
                g = 2 * mb + mt
                for n in range(2):
                    b = next_bank()
                    pairs = [(wtile(s_ga, k, 512, (mb % 2) * 256 + mt * P),
                              XBs(k, n * 512, 512)) for k in range(NT)]
                    mm = S.add("pe", f_mmgroup(pb[b][:, :], pairs),
                               waits=[ld_ga, xb_ld, bank_free[b]], inc="sPE")
                    ev = S.add("act", f_act(gateb(n), pb[b][:, :],
                                            AF.Sigmoid, bias=svc(SV_B_GAT, g)),
                               waits=[("sPE", mm), gate_free[n]], inc="sACT")
                    bank_free[b] = ("sACT", ev)
                    S.add("dve", f_tt(Y0n(g, n), Y0n(g, n), gateb(n), ALU.mult),
                          waits=[("sACT", ev)], inc="sDVE")
                    ag = S.add("dve", f_tt(AGc(g, n), AGc(g, n), gateb(n),
                                           ALU.mult), inc="sDVE")
                    gate_free[n] = ("sDVE", ag)
            if mb == 6:
                S.add("dve", f_tt(SV[:, SV_S0:SV_S0 + 16], SV[:, SV_S0:SV_S0 + 16],
                                  SV[:, SV_MO:SV_MO + 16], ALU.mult),
                      waits=[("sS0", s0_ld[0])], inc="sDVE")
            if mb >= 6:
                for g in (2 * (mb - 6), 2 * (mb - 6) + 1):
                    emit_corr(g)
                    corr_state["todo"].remove(g)
            if mb % 2 == 1:
                mark_slot(s_ga, "sDVE", S.n("sDVE"))

        for g in list(corr_state["todo"]):
            emit_corr(g)
        y_done = S.n("sDVE")

        def ln_rows_and_bcast(last_vmm, last_sqmm, extra_dve=0, scratch=None):
            scr = pS0[0:1, :] if scratch is None else scratch
            r1 = S.add("dve", f_ts(STATR[0:1, 0:512], pS0[0:1, :], 1.0 / D, None,
                                   ALU.mult), waits=[("sPE", last_vmm)], inc="sDVE")
            S.add("dve", f_ts(STATR[0:1, 512:1024], pS1[0:1, :], 1.0 / D, None,
                              ALU.mult), waits=[("sPE", last_sqmm)], inc="sDVE")
            S.add("dve", f_tt(scr, STATR[0:1, 0:512],
                              STATR[0:1, 0:512], ALU.mult), inc="sDVE")
            r4 = S.add("dve", f_tt(STATR[0:1, 512:1024], STATR[0:1, 512:1024],
                                   scr, ALU.subtract), inc="sDVE")
            S.add("act", f_act(scr, STATR[0:1, 512:1024],
                               AF.Ln, bias=SV[0:1, SV_CONST:SV_CONST + 1]),
                  waits=[("sDVE", r4)], inc="sACT")
            r6 = S.add("act", f_act(STATR[0:1, 512:1024], scr, AF.Exp,
                                    scale=SV[0:1, SV_CONST + 1:SV_CONST + 2]),
                       inc="sACT")
            r8 = S.add("dve", f_stt(STATR[0:1, 0:512], STATR[0:1, 0:512], -1.0,
                                    STATR[0:1, 512:1024], ALU.mult, ALU.mult),
                       waits=[("sACT", r6)], inc="sDVE")
            bw = [("sACT", r6), ("sDVE", max(ones_done, extra_dve, r8))]
            bR = S.add("pe", f_mm1(pR[:, :], ONES1[0:1, :], STATR[0:1, 512:1024]),
                       waits=bw, inc="sPE")
            bM = S.add("pe", f_mm1(pM[:, :], ONES1[0:1, :], STATR[0:1, 0:512]),
                       waits=[("sDVE", r8)], inc="sPE")
            return max(bR, bM)

        xr_free = [("sDVE", y_done), ("sDVE", y_done)]
        Vbase = [R0, R1]
        stat_after = [0, 0]
        hb_marks = [[], []]
        x1_done = [0, 0]

        def emit_cpre_step(c, m, sub):
            if m % 4 == 0:
                sub["blk"] = stream_block(w_out_d[m // 4])
            s_o, ld_o = sub["blk"]
            xi = m % 2
            xr = R2 + xi * 512
            xl = dma("sync", arf(xr, 512),
                     xT_d[m * P:(m + 1) * P, c * CH:(c + 1) * CH],
                     f"sXR{xi}", waits=[xr_free[xi]])
            b = next_bank()
            pairs = [(wtile(s_o, k, 512, (m % 4) * P), Y0c(k, c)) for k in range(NT)]
            mm = S.add("pe", f_mmgroup(pb[b][:, :], pairs),
                       waits=[ld_o, bank_free[b], ("sDVE", y_done)], inc="sPE")
            vm = S.add("dve", f_stt(arf(Vbase[c] + m * 512, 512), pb[b][:, :],
                                    svc(SV_B_OUT, m), arf(xr, 512),
                                    ALU.add, ALU.add),
                       waits=[("sPE", mm), (f"sXR{xi}", xl)], inc="sDVE")
            bank_free[b] = ("sDVE", vm)
            xr_free[xi] = ("sDVE", vm)
            sub["v_marks"].append(vm)
            stw = [("sDVE", vm)]
            if stat_after[1 - c]:
                stw.append(("sPE", stat_after[1 - c]))
            S.add("act", f_act(STB(m), arf(Vbase[c] + m * 512, 512), AF.Copy),
                  waits=stw, inc="sACT")
            sq = S.add("act", f_act(STBq(m), arf(Vbase[c] + m * 512, 512), AF.Square),
                       inc="sACT")
            sub["stat_marks"].append(sq)
            if m % 4 == 3:
                mark_slot(s_o, "sPE", mm)

        def emit_cstats(c, sub, extra_dve=0):
            ssum = S.add("pe", f_mmgroup(pS0[0:1, :],
                                         [(ONESB[:, 0:1], STB(k)) for k in range(NT)]),
                         waits=[("sACT", max(sub["stat_marks"])),
                                ("sDVE", ones_done)], inc="sPE")
            ssq = S.add("pe", f_mmgroup(pS1[0:1, :],
                                        [(ONESB[:, 0:1], STBq(k)) for k in range(NT)]),
                        inc="sPE")
            stat_after[c] = ssq
            return ln_rows_and_bcast(ssum, ssq, extra_dve=extra_dve)

        def bfly_pair_cb(la, lb, sp):
            S.add("dve", f_tt(arb(sp, CH), arb(la, CH), arb(lb, CH), ALU.subtract),
                  inc="sDVE")
            S.add("dve", f_tt(arb(la, CH), arb(la, CH), arb(lb, CH), ALU.add),
                  inc="sDVE")

        def h128_pair(la, lb, marks):
            dv = S.n("sDVE")
            be = next_bank()
            S.add("pe", f_mm1(pb[be][:, :], H128[:, :], arb(la, CH),
                              start=True, stop=False),
                  waits=[("sDVE", dv), bank_free[be], sv_ld], inc="sPE")
            S.add("pe", f_mm1(pb[be][:, :], H128[:, :], arb(lb, CH),
                              start=False, stop=True), inc="sPE")
            bo = next_bank()
            S.add("pe", f_mm1(pb[bo][:, :], H128[:, :], arb(la, CH),
                              start=True, stop=False),
                  waits=[bank_free[bo]], inc="sPE")
            mmo = S.add("pe", f_mm1(pb[bo][:, :], H128N[:, :], arb(lb, CH),
                                    start=False, stop=True), inc="sPE")
            eve = S.add("act", f_act(arb(la, CH), pb[be][:, :], AF.Copy),
                        waits=[("sPE", mmo)], inc="sACT")
            evo = S.add("act", f_act(arb(lb, CH), pb[bo][:, :], AF.Copy),
                        inc="sACT")
            bank_free[be] = ("sACT", eve)
            bank_free[bo] = ("sACT", evo)
            marks.append(evo)

        hb_loc = [None, None]

        def make_bfly1(c, spare):
            locs = {g: XB_OFF + g * T + c * CH for g in range(NT)}

            def final_cb(p, la, lb):
                h128_pair(la, lb, hb_marks[c])
            bf = Bfly(NT, locs, spare, bfly_pair_cb, final_cb)
            hb_loc[c] = bf.loc
            return bf

        def emit_dpost_step(c, m, sub, bcast):
            tcol = R2 + 1024
            vt = arf(Vbase[c] + m * 512, 512)
            S.add("dve", f_tt(arf(tcol, 512), vt, pR[:, :], ALU.mult),
                  waits=[("sPE", bcast),
                         ("sACT", max(sub["stat_marks"]))], inc="sDVE")
            t2 = S.add("dve", f_tt(vt, arf(tcol, 512), pM[:, :], ALU.add),
                       inc="sDVE")
            S.add("dve", f_ts(HBs(m, c), vt, svc(SV_SUP, m), None, ALU.mult),
                  inc="sDVE")
            sub["bfly"].tile_arrived(m)
            sub["nsteps"] = sub.get("nsteps", 0) + 1
            if sub["nsteps"] == NT:
                x1_done[c] = S.n("sDVE")

        DPERM = [v for p in zip(range(NT // 2), range(NT // 2, NT)) for v in p]

        sub0 = {"v_marks": [], "stat_marks": [], "bfly": make_bfly1(0, SPARE_OFF)}
        for m in range(NT):
            emit_cpre_step(0, m, sub0)
        bcast0 = emit_cstats(0, sub0)

        sub1 = {"v_marks": [], "stat_marks": [],
                "bfly": make_bfly1(1, SPARE_OFF + CH)}
        for m in range(NT):
            emit_cpre_step(1, m, sub1)
            emit_dpost_step(0, DPERM[m], sub0, bcast0)
        bcast1 = emit_cstats(1, sub1, extra_dve=x1_done[0])

        d1_state = {"m": 0}

        def pump_d1():
            m = d1_state["m"]
            if m < NT:
                emit_dpost_step(1, DPERM[m], sub1, bcast1)
                d1_state["m"] = m + 1
            elif m == NT:
                d1_state["store"] = dma(
                    "sync", x1h_d.ap().rearrange("(m p) t -> p m t", p=P),
                    ARF[:, R1:R1 + 8192].rearrange("p (m t) -> p m t", m=NT),
                    "sX1S", waits=[("sDVE", x1_done[1])])
                d1_state["m"] = m + 1

        stg_free = [("sU0", 0), ("sU1", 0)]
        stg_idx = [0]

        def stg_ap(j):
            return ARF[:, R2 + 1024 + j * 256: R2 + 1024 + (j + 1) * 256]\
                .bitcast(BF16)

        vt_marks0 = []
        locs2 = {i: U2_OFF + i * CH for i in range(NI)}
        bfly2 = Bfly(NI, locs2, sub0["bfly"].stale_slot(), bfly_pair_cb,
                     lambda p, la, lb: h128_pair(la, lb, vt_marks0))
        gt_ring = [("sDVE", S.n("sDVE")), ("sDVE", S.n("sDVE"))]
        gi_ctr = [0]
        last_ug_mm = [0]

        def emit_ug_tile(c, tile, s_ug, ld_ug):
            hb_ok = max(hb_marks[c])
            bu = next_bank()
            pairs_u = [(wtile(s_ug, k, 512, (tile % 2) * P),
                        arb(hb_loc[c][k] + 0, CH)) for k in range(NT)]
            mmup = S.add("pe", f_mmgroup(pb[bu][:, :], pairs_u),
                         waits=[ld_ug, bank_free[bu], ("sACT", hb_ok)], inc="sPE")
            bg = next_bank()
            pairs_g = [(wtile(s_ug, k, 512, 256 + (tile % 2) * P),
                        arb(hb_loc[c][k] + 0, CH)) for k in range(NT)]
            mmgt = S.add("pe", f_mmgroup(pb[bg][:, :], pairs_g),
                         waits=[ld_ug, bank_free[bg], ("sACT", hb_ok)], inc="sPE")
            last_ug_mm[0] = mmgt
            gi = gi_ctr[0] % 2
            gi_ctr[0] += 1
            gcol = R2 + gi * 512
            sg = S.add("act", f_act(arf(gcol, 512), pb[bg][:, :], AF.Silu,
                                    scale=svc(SV_WS_GT, tile)),
                       waits=[("sPE", mmgt), gt_ring[gi], sv_ld], inc="sACT")
            if c == 0:
                u2 = S.add("dve", f_stt(U2s(tile), pb[bu][:, :], svc(SV_WSU2, tile),
                                        arf(gcol, 512), ALU.mult, ALU.mult),
                           waits=[("sPE", mmup), ("sACT", sg), ("sDVE", sv_pre)],
                           inc="sDVE")
            else:
                j = stg_idx[0] % 2
                stg_idx[0] += 1
                u2 = S.add("dve", f_stt(stg_ap(j), pb[bu][:, :], svc(SV_WSU2, tile),
                                        arf(gcol, 512), ALU.mult, ALU.mult),
                           waits=[("sPE", mmup), ("sACT", sg), ("sDVE", sv_pre),
                                  stg_free[j]], inc="sDVE")
                d = tile // 2
                sp = DPOS[d] if tile % 2 == 0 else 32 + DPOS[d]
                n = dma("sync", u2h_d[sp], stg_ap(j), f"sU{j}",
                        waits=[("sDVE", u2)])
                stg_free[j] = (f"sU{j}", n)
            bank_free[bu] = ("sDVE", u2)
            bank_free[bg] = ("sDVE", u2)
            gt_ring[gi] = ("sDVE", u2)
            if c == 0:
                bfly2.tile_arrived(tile)
            return u2

        border = [b for pr in zip(range(NI // 4), range(NI // 4, NI // 2))
                  for b in pr]
        DORDER = [d for pr in zip(range(16), range(16, 32)) for d in pr]
        DPOS = {d: i for i, d in enumerate(DORDER)}
        for pos in range(32):
            b = border[pos]
            if pos == 8:
                while d1_state["m"] <= NT:
                    pump_d1()
            s_ug, ld_ug = stream_block(w_ug_d[b])
            for tile in (2 * b, 2 * b + 1):
                emit_ug_tile(0, tile, s_ug, ld_ug)
                if pos >= 8:
                    emit_ug_tile(1, tile, s_ug, ld_ug)
            mark_slot(s_ug, "sDVE", S.n("sDVE"))
            if pos < 8:
                pump_d1()
                pump_d1()
        for b in border[:8]:
            s_ug, ld_ug = stream_block(w_ug_d[b])
            for tile in (2 * b, 2 * b + 1):
                emit_ug_tile(1, tile, s_ug, ld_ug)
            mark_slot(s_ug, "sDVE", S.n("sDVE"))
        vt_ok0 = max(vt_marks0)
        init_spare0 = sub0["bfly"].stale_slot()
        reloc_mark = 0
        for t, lo in list(bfly2.loc.items()):
            if lo == init_spare0:
                stale = bfly2.stale_slot()
                reloc_mark = S.add(
                    "dve", f_copy(arb(stale, CH), arb(lo, CH)),
                    waits=[("sACT", vt_ok0)], inc="sDVE")
                bfly2.loc[t] = stale
        u2_last_mark = S.n("sDVE")

        def emit_down_ln2(c, loc_of, vt_ok, x1_ready, per_m_pre=None,
                          f_extra=(), mm_extra=()):
            f_free = ("sDVE", 0)
            sq_free = [("sPE", 0), ("sPE", 0)]
            v2_marks = []
            last_vmm = last_sqmm = 0
            down_last = [0]
            for m in range(NT):
                s_dn, ld_dn = stream_block(w_dn_d[m])
                b = next_bank()
                pairs = [(wtile(s_dn, k, 128, 0), arb(loc_of[k], CH))
                         for k in range(NI)]
                mm = S.add("pe", f_mmgroup(pb[b][:, :], pairs),
                           waits=[ld_dn, bank_free[b], ("sACT", vt_ok)]
                                 + list(mm_extra), inc="sPE")
                down_last[0] = mm
                extra = list(per_m_pre(m)) if per_m_pre is not None else []
                fi = m % 2
                fcol = R2 + 1024
                fw = [("sPE", mm), f_free]
                if m < 1:
                    fw.append(("sDVE", S.n("sDVE")))
                    fw += list(f_extra)
                fe = S.add("act", f_act(arf(fcol, 512), pb[b][:, :], AF.Copy,
                                        scale=svc(SV_WS_DP, m)), waits=fw, inc="sACT")
                bank_free[b] = ("sACT", fe)
                v2w = [("sACT", fe), x1_ready] + extra
                src = R0 if c == 0 else R1
                v2 = S.add("dve", f_tt(arf(R1 + m * 512, 512),
                                       arf(src + m * 512, 512),
                                       arf(fcol, 512), ALU.add),
                           waits=v2w, inc="sDVE")
                f_free = ("sDVE", v2)
                v2_marks.append(v2)
                last_vmm = S.add("pe", f_mm1(pS0[0:1, :], ONESF[:, 0:1],
                                             arf(R1 + m * 512, 512),
                                             start=(m == 0), stop=(m == NT - 1)),
                                 waits=[("sDVE", v2), ("sDVE", ones_done)], inc="sPE")
                sqw = [("sDVE", v2), sq_free[fi]]
                if m < 2:
                    sqw.append(("sDVE", u2_last_mark))
                sq = S.add("act", f_act(arf(R2 + fi * 512, 512),
                                        arf(R1 + m * 512, 512), AF.Square),
                           waits=sqw, inc="sACT")
                last_sqmm = S.add("pe", f_mm1(pS1[0:1, :], ONESF[:, 0:1],
                                              arf(R2 + fi * 512, 512),
                                              start=(m == 0), stop=(m == NT - 1)),
                                  waits=[("sACT", sq)], inc="sPE")
                sq_free[fi] = ("sPE", last_sqmm)
                mark_slot(s_dn, "sPE", mm)
            return down_last[0], v2_marks, last_vmm, last_sqmm

        def emit_norm_step(c, m, bcast, v2_all, store_eng, obase=R1):
            t1 = S.add("dve", f_tt(arf(obase + m * 512, 512),
                                   arf(R1 + m * 512, 512),
                                   pR[:, :], ALU.mult),
                       waits=[("sPE", bcast), ("sDVE", v2_all)], inc="sDVE")
            t2 = S.add("dve", f_tt(arf(obase + m * 512, 512),
                                   arf(obase + m * 512, 512),
                                   pM[:, :], ALU.add), inc="sDVE")
            stn = dma(store_eng, out_d[m * P:(m + 1) * P, c * CH:(c + 1) * CH],
                      arf(obase + m * 512, 512), f"sST{m % 2}",
                      waits=[("sDVE", t2)])
            return t1, t2, stn

        ev_locs = {d: XB_OFF + DPOS[d] * 512 for d in range(32)}
        od_locs = {d: U2_OFF + DPOS[d] * 512 for d in range(32)}
        ev_final = {}
        vt_marks1 = []
        loc1_final = {}

        def ev_done(d, loc, _):
            ev_final[d] = loc

        def od_done(d, loc, _):
            la = ev_final[d]
            h128_pair(la, loc, vt_marks1)
            loc1_final[2 * d] = la
            loc1_final[2 * d + 1] = loc
        bfe = Bfly(32, ev_locs, SPARE_OFF, bfly_pair_cb, ev_done, absorb_h1=False)
        bfo = Bfly(32, od_locs, SPARE_OFF + 512, bfly_pair_cb, od_done,
                   absorb_h1=False)

        ev_dmas = []
        for g in range(4):
            n = dma("sync",
                    ARB[:, XB_OFF + g * 8 * 512: XB_OFF + (g + 1) * 8 * 512]
                    .rearrange("p (i t) -> p i t", i=8),
                    u2h_d[g * 8:(g + 1) * 8].rearrange("i p t -> p i t"),
                    f"sU{g % 2}",
                    waits=[("sPE", last_ug_mm[0]),
                           ("sU0", S.n("sU0")), ("sU1", S.n("sU1"))])
            ev_dmas.append((f"sU{g % 2}", n))
        ev_state = {"g": 0}

        def pump_even(m):
            if m % 4 == 0 and ev_state["g"] < 4:
                g = ev_state["g"]
                ev_state["g"] = g + 1
                S.add("dve", lambda e: e.nop(), waits=[ev_dmas[g]], inc="sDVE")
                for i in range(g * 8, (g + 1) * 8):
                    bfe.tile_arrived(DORDER[i])
            return []

        x1_ready0 = ("sX1S", d1_state["store"])
        dn0_last, v2m0, lvmm0, lsqm0 = emit_down_ln2(
            0, bfly2.loc, vt_ok0, x1_ready0, per_m_pre=pump_even,
            f_extra=[stg_free[0], stg_free[1]], mm_extra=[("sDVE", reloc_mark)])

        od_dmas = []
        for g in range(4):
            n = dma("pool",
                    ARB[:, U2_OFF + g * 8 * 512: U2_OFF + (g + 1) * 8 * 512]
                    .rearrange("p (i t) -> p i t", i=8),
                    u2h_d[32 + g * 8: 32 + (g + 1) * 8].rearrange("i p t -> p i t"),
                    f"sU{g % 2}",
                    waits=[("sPE", dn0_last),
                           (f"sU{g % 2}", S.n(f"sU{g % 2}"))])
            S.add("dve", lambda e: e.nop(), waits=[(f"sU{g % 2}", n)], inc="sDVE")
            for i in range(g * 8, (g + 1) * 8):
                bfo.tile_arrived(DORDER[i])
        vt_ok1 = max(vt_marks1)

        bcast0 = ln_rows_and_bcast(lvmm0, lsqm0, extra_dve=x1_done[1],
                                   scratch=pR[0:1, :])
        x1slab = {}

        def emit_x1_slab(m):
            t1, t2, _ = emit_norm_step(0, m, bcast0, max(v2m0), "sync", obase=R0)
            n = dma("sync", arf(R1 + m * 512, 512),
                    x1h_d[m * P:(m + 1) * P, :], "sX1",
                    waits=[("sDVE", t1), ("sX1", S.n("sX1")),
                           ("sX1S", d1_state["store"])])
            x1slab[m] = ("sX1", n)

        emit_x1_slab(0)
        emit_x1_slab(1)

        def pre_p1(m):
            if m + 2 < NT:
                emit_x1_slab(m + 2)
            return [x1slab[m]]

        dn1_last, v2m1, lvmm1, lsqm1 = emit_down_ln2(
            1, loc1_final, vt_ok1, ("sX1", 0), per_m_pre=pre_p1)

        bcast1 = ln_rows_and_bcast(lvmm1, lsqm1)
        for m in range(NT):
            emit_norm_step(1, m, bcast1, max(v2m1), "sync")

        S.add("sync", lambda e: e.nop(),
              waits=[("sST0", S.n("sST0")), ("sST1", S.n("sST1"))])

        @block.sync
        def _(sy):
            S.replay("sync", sy, sems)

        @block.gpsimd
        def _(g):
            S.replay("pool", g, sems)

        @block.tensor
        def _(t):
            S.replay("pe", t, sems)

        @block.scalar
        def _(sc):
            S.replay("act", sc, sems)

        @block.vector
        def _(v):
            S.replay("dve", v, sems)

    return nc


def _vec16(v):
    return np.ascontiguousarray(v.reshape(16, P).T)


def _vec64(v):
    return np.ascontiguousarray(v.reshape(64, P).T)


def _blocks(W, nblk, width):
    K, M = W.shape
    assert M == nblk * width
    return np.ascontiguousarray(W.reshape(K, nblk, width).transpose(1, 0, 2))


_NC_CACHE = {}


def kernel(L0_attn_x, L0_attn_W_in, L0_attn_b_in, L0_attn_W_decay, L0_attn_b_decay,
           L0_attn_W_gate, L0_attn_b_gate, L0_attn_W_out, L0_attn_b_out,
           L0_attn_decay_bias, L0_ffn_SU_ug, L0_ffn_SU_dp, L0_ffn_Wscale_ug,
           L0_ffn_Wscale_dp, L0_ffn_W_up, L0_ffn_W_gate, L0_ffn_W_down):
    x = np.asarray(L0_attn_x, np.float32)
    f = lambda a: np.asarray(a, np.float32)

    bi_ = _blocks(f(L0_attn_W_in), 8, 256)
    bd_ = _blocks(f(L0_attn_W_decay), 8, 256)
    w_ind = np.ascontiguousarray(np.concatenate([bi_, bd_], axis=2))
    w_gat = _blocks(f(L0_attn_W_gate), 4, 512)
    w_out = _blocks(f(L0_attn_W_out), 4, 512)
    bu2 = _blocks(np.ascontiguousarray(f(L0_ffn_W_up).T), 32, 256)
    bg2 = _blocks(np.ascontiguousarray(f(L0_ffn_W_gate).T), 32, 256)
    w_ug = np.ascontiguousarray(np.concatenate([bu2, bg2], axis=2))
    w_dn = _blocks(np.ascontiguousarray(f(L0_ffn_W_down).T), 16, 128)

    h128 = hadamard(P).astype(ml_dtypes.bfloat16)
    h128n = (-hadamard(P)).astype(ml_dtypes.bfloat16)

    sv_base = np.zeros((P, SV_W), np.float32)
    sv_base[:, SV_B_IN:SV_B_IN + 16] = _vec16(f(L0_attn_b_in))
    sv_base[:, SV_B_DEC:SV_B_DEC + 16] = _vec16(f(L0_attn_b_decay))
    sv_base[:, SV_DBIAS:SV_DBIAS + 16] = _vec16(f(L0_attn_decay_bias))
    sv_base[:, SV_B_GAT:SV_B_GAT + 16] = _vec16(f(L0_attn_b_gate))
    sv_base[:, SV_B_OUT:SV_B_OUT + 16] = _vec16(f(L0_attn_b_out))
    sv_base[:, SV_SU_UG:SV_SU_UG + 16] = _vec16(f(L0_ffn_SU_ug))
    sv_base[:, SV_WS_DP:SV_WS_DP + 16] = _vec16(f(L0_ffn_Wscale_dp))
    sv_base[:, SV_WS_UP:SV_WS_UP + 64] = _vec64(f(L0_ffn_Wscale_ug)[:I])
    sv_base[:, SV_WS_GT:SV_WS_GT + 64] = _vec64(f(L0_ffn_Wscale_ug)[I:])
    sv_base[:, SV_SU_DP:SV_SU_DP + 64] = _vec64(f(L0_ffn_SU_dp))
    sv_base[:, SV_CONST] = EPS
    sv_base[:, SV_CONST + 1] = -0.5

    if "prog" not in _NC_CACHE:
        _NC_CACHE["prog"] = build_program()
    nc = _NC_CACHE["prog"]

    in_maps = []
    for c in range(8):
        b, h = c // 2, c % 2
        xT = np.ascontiguousarray(x[b, h * T:(h + 1) * T, :].T)
        sv = sv_base.copy()
        sv[:, SV_ME:SV_ME + 16] = 1.0 if h == 0 else 0.0
        sv[:, SV_MO:SV_MO + 16] = 1.0 if h == 1 else 0.0
        in_maps.append({
            "xT": xT, "w_ind": w_ind, "w_gat": w_gat, "w_out": w_out,
            "w_ug": w_ug, "w_dn": w_dn, "sv": sv, "h128": h128,
            "h128n": h128n,
        })

    res = run_bass_kernel_spmd(nc, in_maps, core_ids=list(range(8)))

    out = np.empty((4, 2048, D), np.float32)
    for c in range(8):
        b, h = c // 2, c % 2
        out[b, h * T:(h + 1) * T, :] = res.results[c]["out"].T
    return out


if __name__ == "__main__":
    import reference
    ins = reference.setup_inputs()
    got = kernel(**{k: np.asarray(v) for k, v in ins.items()})
    exp = np.asarray(reference.reference(**ins))
    err = np.abs(got - exp)
    rel = np.sqrt(((got - exp) ** 2).mean() / (exp ** 2).mean())
    print("maxabs", err.max(), "relL2", rel)
